# version 05 (final)
# baseline (speedup 1.0000x reference)
import contextlib
import math
import numpy as np
import ml_dtypes
import concourse.bass as bass
import concourse.mybir as mybir
from concourse.bass_utils import run_bass_kernel_spmd

F32 = mybir.dt.float32
BF16 = mybir.dt.bfloat16
AF = mybir.ActivationFunctionType
ALU = mybir.AluOpType
AX = mybir.AxisListType
bfd = ml_dtypes.bfloat16

NCORE = 8
B_PER = 2
S = 2048
T = B_PER * S
D = 1024


def bf(a):
    return np.asarray(a).astype(bfd)


def f32(a):
    return np.ascontiguousarray(np.asarray(a), dtype=np.float32)


class Seq:

    def __init__(self, nc, stack):
        self.nc = nc
        self.sems = [stack.enter_context(nc.semaphore(name=f"seqs{i}")) for i in range(4)]
        self.cur = 0
        self.total = 0
        self.LIMIT = 24000
        self.eng = dict(pe=nc.tensor, ve=nc.vector, act=nc.scalar, sp=nc.sync, pool=nc.gpsimd)
        self.cpool = stack.enter_context(nc.sbuf_tensor("cpool", [128, 16], F32))
        self.cvals = {}

    def op(self, engname, meth, *args, dma=False, **kw):
        eng = self.eng[engname]
        inc = 16 if dma else 1
        if self.total + inc > self.LIMIT:
            nxt = (self.cur + 1) % len(self.sems)
            eng.wait_ge(self.sems[self.cur], self.total)
            eng.sem_clear(self.sems[nxt])
            self.cur = nxt
            self.total = 0
        eng.wait_ge(self.sems[self.cur], self.total)
        inst = getattr(eng, meth)(*args, **kw)
        inst.then_inc(self.sems[self.cur], inc)
        self.total += inc
        return inst

    def dma(self, out, in_):
        return self.op('sp', 'dma_start', out, in_, dma=True)

    def cb(self, val, p=128):
        if not hasattr(self, 'cpool'):
            raise RuntimeError
        if val not in self.cvals:
            idx = len(self.cvals)
            self.cvals[val] = idx
            self.op('ve', 'memset', self.cpool[:, idx:idx + 1], float(val))
        return self.cpool[0:p, self.cvals[val]:self.cvals[val] + 1]

    def mm(self, out, lhsT, rhs, start=True, stop=True):
        return self.op('pe', 'matmul', out, lhsT, rhs, start=start, stop=stop)

    def fin(self):
        self.eng['sp'].wait_ge(self.sems[self.cur], self.total)


class St:

    def __init__(self, nc):
        self.nc = nc
        self.inputs = []
        self.outputs = []

    def inp(self, name, shape, dt=F32):
        self.inputs.append(name)
        return self.nc.dram_tensor(name, list(shape), dt, kind="ExternalInput").ap()

    def out(self, name, shape, dt=F32):
        self.outputs.append(name)
        return self.nc.dram_tensor(name, list(shape), dt, kind="ExternalOutput").ap()

    def internal(self, name, shape, dt=F32):
        return self.nc.dram_tensor(name, list(shape), dt, kind="Internal").ap()


def softplus_ops(s, out, x, tmp1, tmp2, p=128):
    s.op('act', 'activation', tmp1, x, AF.Abs)
    s.op('act', 'activation', tmp1, tmp1, AF.Exp, scale=-1.0)
    s.op('act', 'activation', tmp1, tmp1, AF.Ln, bias=s.cb(1.0, p))
    s.op('act', 'activation', tmp2, x, AF.Relu)
    s.op('ve', 'tensor_tensor', out, tmp1, tmp2, ALU.add)


def make_statics():
    st = {}
    st['id128f'] = f32(np.eye(128))
    st['id128b'] = bf(np.eye(128))
    st['ones1x128b'] = bf(np.ones((1, 128)))
    st['ones1x128f'] = f32(np.ones((1, 128)))
    st['onescol128b'] = bf(np.ones((128, 1)))
    blk = np.zeros((128, 2), np.float32)
    blk[0:64, 0] = 1.0
    blk[64:128, 1] = 1.0
    st['blk64b'] = bf(blk)
    st['blk64Tb'] = bf(blk.T)
    for sh in (0, 128, 256, 384):
        m = np.zeros((128, 512), np.float32)
        for p in range(128):
            lo = p + sh
            if lo < 512:
                m[p, lo:] = 1.0
        st[f'maskL1_{sh}'] = bf(m)
        st[f'maskL1n_{sh}'] = f32((m - 1.0) * 1e30)
    for sh in (0, 128, 256, 384):
        m = np.zeros((128, 512), np.float32)
        for p in range(128):
            hi = p + sh
            m[p, 0:min(512, hi + 1)] = 1.0
        st[f'maskL3_{sh}'] = f32(m)
        st[f'maskL3n_{sh}'] = f32((m - 1.0) * 1e9)
    for sh in (0, 128, 256, 384):
        m = np.zeros((128, 512), np.float32)
        for p in range(128):
            lo = p + sh
            if lo < 512:
                m[p, lo:] = 1.0
        st[f'maskL3T_{sh}'] = bf(m)
    def bucket(d):
        nb = 16
        rb = nb if d > 0 else 0
        ad = abs(d)
        if ad < 8:
            return rb + ad
        v = 8 + int(math.log(ad / 8) / math.log(128 / 8) * 8)
        return rb + min(v, 15)
    zr = np.zeros((32, 4095), np.float32)
    for i in range(4095):
        zr[bucket(2047 - i), i] = 1.0
    st['Zrev'] = zr
    p64 = np.zeros((6, 64), np.float32)
    for e in range(64):
        for i in range(6):
            p64[i, e] = 1.0 if (e >> i) & 1 else -1.0
    st['P64T'] = p64
    return st


STATICS = make_statics()


def b_ln(names_in, name_xsum, name_xn, name_xnb):
    nc = bass.Bass("TRN2", target_bir_lowering=False)
    stc = St(nc)
    ins = [stc.inp(n, [D, T]) for n in names_in]
    xsum = stc.out(name_xsum, [D, T]) if name_xsum else None
    xn = stc.out(name_xn, [D, T])
    xnb = stc.out(name_xnb, [D, T], BF16)
    with contextlib.ExitStack() as stack:
        e = stack.enter_context
        xts = e(nc.sbuf_tensor("xts", [128, 8 * 512], F32))
        xbts = e(nc.sbuf_tensor("xbts", [128, 8 * 512], BF16))
        sqb = e(nc.sbuf_tensor("sqb", [128, 512], BF16))
        tmp = e(nc.sbuf_tensor("tmp", [128, 512], F32))
        o1 = e(nc.sbuf_tensor("o1", [1, 128], BF16))
        rowm = e(nc.sbuf_tensor("rowm", [1, 512], F32))
        rows = e(nc.sbuf_tensor("rows", [1, 512], F32))
        rowmb = e(nc.sbuf_tensor("rowmb", [1, 512], BF16))
        rowrb = e(nc.sbuf_tensor("rowrb", [1, 512], BF16))
        onecol = e(nc.sbuf_tensor("onecol", [128, 1], BF16))
        psm = e(nc.psum_tensor("psm", [1, 512], F32))
        pss = e(nc.psum_tensor("pss", [1, 512], F32))
        psmb = e(nc.psum_tensor("psmb", [128, 512], F32))
        psrb = e(nc.psum_tensor("psrb", [128, 512], F32))
        s = Seq(nc, stack)
        s.op('ve', 'memset', onecol[:], 1.0)
        s.op('ve', 'memset', o1[:], 1.0)
        for ts in range(8):
            c0 = ts * 512
            for dt in range(8):
                sl = xts[:, dt * 512:(dt + 1) * 512]
                s.dma(sl, ins[0][dt * 128:(dt + 1) * 128, c0:c0 + 512])
                for k in range(1, len(ins)):
                    s.dma(tmp[:], ins[k][dt * 128:(dt + 1) * 128, c0:c0 + 512])
                    s.op('ve', 'tensor_tensor', sl, sl, tmp[:], ALU.add)
                if xsum is not None:
                    s.dma(xsum[dt * 128:(dt + 1) * 128, c0:c0 + 512], sl)
                s.op('ve', 'tensor_copy', xbts[:, dt * 512:(dt + 1) * 512], sl)
            for dt in range(8):
                s.mm(psm[:], onecol[:], xbts[:, dt * 512:(dt + 1) * 512],
                     start=(dt == 0), stop=(dt == 7))
            for dt in range(8):
                s.op('act', 'activation', sqb[:], xts[:, dt * 512:(dt + 1) * 512], AF.Square)
                s.mm(pss[:], onecol[:], sqb[:], start=(dt == 0), stop=(dt == 7))
            s.op('act', 'mul', rowm[:], psm[:], 1.0 / 1024.0)
            s.op('act', 'mul', rows[:], pss[:], 1.0 / 1024.0)
            s.op('ve', 'tensor_tensor', tmp[0:1, :], rowm[:], rowm[:], ALU.mult)
            s.op('ve', 'tensor_tensor', rows[:], rows[:], tmp[0:1, :], ALU.subtract)
            s.op('act', 'activation', rows[:], rows[:], AF.Ln, bias=s.cb(1e-5, 1))
            s.op('act', 'activation', rows[:], rows[:], AF.Exp, scale=-0.5)
            s.op('ve', 'tensor_copy', rowmb[:], rowm[:])
            s.op('ve', 'tensor_copy', rowrb[:], rows[:])
            s.mm(psmb[:], o1[:], rowmb[:])
            s.mm(psrb[:], o1[:], rowrb[:])
            for dt in range(8):
                sl = xts[:, dt * 512:(dt + 1) * 512]
                s.op('ve', 'tensor_tensor', sl, sl, psmb[:], ALU.subtract)
                s.op('ve', 'tensor_tensor', sl, sl, psrb[:], ALU.mult)
                s.dma(xn[dt * 128:(dt + 1) * 128, c0:c0 + 512], sl)
                s.op('ve', 'tensor_copy', xbts[:, dt * 512:(dt + 1) * 512], sl)
                s.dma(xnb[dt * 128:(dt + 1) * 128, c0:c0 + 512],
                      xbts[:, dt * 512:(dt + 1) * 512])
        s.fin()
    return nc, stc




def b_l0a():
    nc = bass.Bass("TRN2", target_bir_lowering=False)
    stc = St(nc)
    xnb = stc.inp("xn0b", [D, T], BF16)
    wq = stc.inp("l0_wq", [D, D], BF16)
    wk = stc.inp("l0_wk", [D, D], BF16)
    wv = stc.inp("l0_wv", [D, D], BF16)
    wo = stc.inp("l0_wo", [D, D], BF16)
    table = stc.inp("l0_table", [32, 16], F32)
    zrev = stc.inp("Zrev", [32, 4095], F32)
    o1b = stc.inp("ones1x128b", [1, 128], BF16)
    ocb = stc.inp("onescol128b", [128, 1], BF16)
    idb = stc.inp("id128b", [128, 128], BF16)
    at0 = stc.out("at0f", [D, T])
    qh = stc.internal("q0b", [D, T], BF16)
    kh = stc.internal("k0b", [D, T], BF16)
    vh = stc.internal("v0b", [D, T], BF16)
    bdr = stc.internal("bdr", [16, 4095], F32)
    with contextlib.ExitStack() as stack:
        e = stack.enter_context
        wsb = e(nc.sbuf_tensor("wsb", [128, 8 * 1024], BF16))
        xsb = e(nc.sbuf_tensor("xsb", [128, 8 * 512], BF16))
        osb = e(nc.sbuf_tensor("osb", [128, 512], BF16))
        osf = e(nc.sbuf_tensor("osf", [128, 512], F32))
        ps0 = e(nc.psum_tensor("ps0", [128, 512], F32))
        psg = e(nc.psum_tensor("psg", [16, 512], F32))
        psd = e(nc.psum_tensor("psd", [1, 512], F32))
        psc = e(nc.psum_tensor("psc", [64, 512], F32))
        psb = e(nc.psum_tensor("psb", [64, 512], F32))
        pst = e(nc.psum_tensor("pst", [128, 64], BF16))
        tabs = e(nc.sbuf_tensor("tabs", [32, 16], F32))
        zrs = e(nc.sbuf_tensor("zrs", [32, 512], F32))
        bdrs = e(nc.sbuf_tensor("bdrs", [16, 512], F32))
        strip = e(nc.sbuf_tensor("strip", [128, 16 * 1152], F32))
        crf = e(nc.sbuf_tensor("crf", [1, 32], F32))
        crow = e(nc.sbuf_tensor("crow", [1, 32], BF16))
        ccol = e(nc.sbuf_tensor("ccol", [128, 32], F32))
        qs = e(nc.sbuf_tensor("qs", [64, 2048], BF16))
        ks = e(nc.sbuf_tensor("ks", [64, 2048], BF16))
        vs = e(nc.sbuf_tensor("vs", [64, 2048], BF16))
        vT = e(nc.sbuf_tensor("vT", [128, 16 * 64], BF16))
        zT = e(nc.sbuf_tensor("zT", [128, 16 * 512], BF16))
        sadd = e(nc.sbuf_tensor("sadd", [128, 512], F32))
        rrf = e(nc.sbuf_tensor("rrf", [1, 512], F32))
        rr = e(nc.sbuf_tensor("rr", [1, 512], BF16))
        rbs = e(nc.sbuf_tensor("rbs", [64, 512], F32))
        ctx = e(nc.sbuf_tensor("ctx", [128, 8 * 2048], BF16))
        ids = e(nc.sbuf_tensor("ids", [128, 128], BF16))
        onesb = e(nc.sbuf_tensor("onesb", [1, 128], BF16))
        occ = e(nc.sbuf_tensor("occ", [128, 1], BF16))
        s = Seq(nc, stack)
        s.dma(ids[:], idb)
        s.dma(onesb[:], o1b)
        s.dma(occ[:], ocb)
        s.dma(tabs[:], table)
        for wten, dst in ((wq, qh), (wk, kh), (wv, vh)):
            for kt in range(8):
                s.dma(wsb[:, kt * 1024:(kt + 1) * 1024], wten[kt * 128:(kt + 1) * 128, :])
            for ts in range(8):
                c0 = ts * 512
                for kt in range(8):
                    s.dma(xsb[:, kt * 512:(kt + 1) * 512],
                          xnb[kt * 128:(kt + 1) * 128, c0:c0 + 512])
                for ot in range(8):
                    for kt in range(8):
                        s.mm(ps0[:], wsb[:, kt * 1024 + ot * 128: kt * 1024 + (ot + 1) * 128],
                             xsb[:, kt * 512:(kt + 1) * 512], start=(kt == 0), stop=(kt == 7))
                    s.op('act', 'copy', osb[:], ps0[:])
                    s.dma(dst[ot * 128:(ot + 1) * 128, c0:c0 + 512], osb[:])
        for i in range(8):
            w = 512 if i < 7 else 4095 - 7 * 512
            s.dma(zrs[:, 0:w], zrev[:, i * 512:i * 512 + w])
            s.mm(psg[:, 0:w], tabs[:], zrs[:, 0:w])
            s.op('act', 'copy', bdrs[:, 0:w], psg[:, 0:w])
            s.dma(bdr[:, i * 512:i * 512 + w], bdrs[:, 0:w])
        s.op('sp', 'dma_start', crf[0:1, 0:16], bass.AP(bdr.tensor, 2176, [[1, 1], [4095, 16]]),
             dma=True, allow_slow_non_contiguous=True)
        s.op('sp', 'dma_start', crf[0:1, 16:32], bass.AP(bdr.tensor, 1918, [[1, 1], [4095, 16]]),
             dma=True, allow_slow_non_contiguous=True)
        s.op('ve', 'tensor_copy', crow[:], crf[:])
        s.mm(ps0[:, 0:32], onesb[:], crow[:])
        s.op('act', 'copy', ccol[:], ps0[:, 0:32])
        for p in range(128):
            s.dma(strip[p:p + 1, :], bass.AP(bdr.tensor, 1535 - p, [[1, 1], [4095, 16], [1, 1152]]))
        for b in range(B_PER):
            cb = b * S
            for h in range(16):
                r0 = h * 64
                s.dma(qs[:], qh[r0:r0 + 64, cb:cb + S])
                s.dma(ks[:], kh[r0:r0 + 64, cb:cb + S])
                s.dma(vs[:], vh[r0:r0 + 64, cb:cb + S])
                for tk in range(16):
                    s.op('pe', 'transpose', pst[:], vs[:, tk * 128:(tk + 1) * 128], ids[0:64, 0:64])
                    s.op('act', 'copy', vT[:, tk * 64:(tk + 1) * 64], pst[:])
                for tqs in range(4):
                    q0 = tqs * 512
                    for tk in range(16):
                        s.mm(ps0[:], ks[:, tk * 128:(tk + 1) * 128], qs[:, q0:q0 + 512])
                        db = tk - 4 * tqs
                        zsl = zT[:, tk * 512:(tk + 1) * 512]
                        if -1 <= db <= 4:
                            col0 = h * 1152 + 512 - 128 * db
                            s.op('ve', 'tensor_tensor', sadd[:], ps0[:],
                                 strip[:, col0:col0 + 512], ALU.add)
                            s.op('act', 'activation', zsl, sadd[:], AF.Exp)
                        else:
                            side = 0 if db <= -2 else 16
                            s.op('act', 'activation', zsl, ps0[:], AF.Exp,
                                 bias=ccol[:, side + h:side + h + 1])
                        s.mm(psd[:], occ[:], zsl, start=(tk == 0), stop=(tk == 15))
                    s.op('ve', 'reciprocal', rrf[:], psd[:])
                    s.op('ve', 'tensor_copy', rr[:], rrf[:])
                    for tk in range(16):
                        s.mm(psc[:], vT[:, tk * 64:(tk + 1) * 64],
                             zT[:, tk * 512:(tk + 1) * 512], start=(tk == 0), stop=(tk == 15))
                    s.mm(psb[:], onesb[0:1, 0:64], rr[:])
                    s.op('act', 'copy', rbs[:], psb[:])
                    ct = h // 2
                    rr0 = (h % 2) * 64
                    s.op('ve', 'tensor_tensor',
                         ctx[rr0:rr0 + 64, ct * 2048 + q0: ct * 2048 + q0 + 512],
                         psc[:], rbs[:], ALU.mult)
            for kt in range(8):
                s.dma(wsb[:, kt * 1024:(kt + 1) * 1024], wo[kt * 128:(kt + 1) * 128, :])
            for ts in range(4):
                c0 = ts * 512
                for ot in range(8):
                    for kt in range(8):
                        s.mm(ps0[:], wsb[:, kt * 1024 + ot * 128: kt * 1024 + (ot + 1) * 128],
                             ctx[:, kt * 2048 + c0: kt * 2048 + c0 + 512],
                             start=(kt == 0), stop=(kt == 7))
                    s.op('act', 'copy', osf[:], ps0[:])
                    s.dma(at0[ot * 128:(ot + 1) * 128, cb + c0: cb + c0 + 512], osf[:])
        s.fin()
    return nc, stc


def b_l0f():
    nc = bass.Bass("TRN2", target_bir_lowering=False)
    stc = St(nc)
    xnf = stc.inp("xn0f", [D, T])
    xnb = stc.inp("xn0b", [D, T], BF16)
    hyp = stc.inp("l0_hyp", [D, 6], F32)
    wex = stc.inp("l0_wex", [64, D * D], BF16)
    p64 = stc.inp("P64T", [6, 64], F32)
    idf = stc.inp("id128f", [128, 128], F32)
    ff0 = stc.out("ff0f", [D, T])
    wall = stc.internal("wall", [32, D * D], BF16)
    NC = 32
    with contextlib.ExitStack() as stack:
        e = stack.enter_context
        xf = e(nc.sbuf_tensor("xf", [128, T], F32))
        emb = e(nc.sbuf_tensor("emb", [128, 8 * 32], BF16))
        embf = e(nc.sbuf_tensor("embf", [128, 32], F32))
        hyps = e(nc.sbuf_tensor("hyps", [128, 8 * 6], F32))
        pr = e(nc.sbuf_tensor("pr", [32, 6], F32))
        sgn = e(nc.sbuf_tensor("sgn", [32, 6], F32))
        sgn2 = e(nc.sbuf_tensor("sgn2", [32, 6], F32))
        ab6 = e(nc.sbuf_tensor("ab6", [32, 6], F32))
        mn = e(nc.sbuf_tensor("mn", [32, 1], F32))
        w6 = e(nc.sbuf_tensor("w6", [32, 6], F32))
        sT = e(nc.sbuf_tensor("sT", [6, 32], F32))
        oh = e(nc.sbuf_tensor("oh", [32, 64], F32))
        ohb = e(nc.sbuf_tensor("ohb", [32, 64], BF16))
        ohT = e(nc.sbuf_tensor("ohT", [64, 32], BF16))
        p64s = e(nc.sbuf_tensor("p64s", [6, 64], F32))
        ids = e(nc.sbuf_tensor("ids", [128, 128], F32))
        wxs = e(nc.sbuf_tensor("wxs", [64, 8192], BF16))
        wob = e(nc.sbuf_tensor("wob", [32, 8192], BF16))
        wsel = e(nc.sbuf_tensor("wsel", [128, 8 * 1024], BF16))
        xcb = e(nc.sbuf_tensor("xcb", [128, 8 * 128], BF16))
        of = e(nc.sbuf_tensor("of", [128, 128], F32))
        psp = e(nc.psum_tensor("psp", [32, 64], F32))
        pst = e(nc.psum_tensor("pst", [64, 64], F32))
        ps0 = e(nc.psum_tensor("ps0", [128, 512], F32))
        psy = e(nc.psum_tensor("psy", [128, 128], F32))
        s = Seq(nc, stack)
        s.dma(ids[:], idf)
        s.dma(p64s[:], p64)
        for dt in range(8):
            s.dma(xf[:], xnf[dt * 128:(dt + 1) * 128, :])
            xr = xf[:].rearrange("p (c l) -> p c l", c=NC)
            s.op('ve', 'tensor_reduce', embf[:], xr, AX.X, ALU.add)
            s.op('act', 'mul', embf[:], embf[:], 1.0 / 128.0)
            s.dma(hyps[:, dt * 6:(dt + 1) * 6], hyp[dt * 128:(dt + 1) * 128, :])
            s.mm(psp[0:32, 0:6], embf[:], hyps[:, dt * 6:(dt + 1) * 6],
                 start=(dt == 0), stop=(dt == 7))
        s.op('act', 'copy', pr[:], psp[0:32, 0:6])
        s.op('act', 'activation', sgn[:], pr[:], AF.Sign)
        s.op('act', 'activation', ab6[:], pr[:], AF.Abs)
        s.op('ve', 'tensor_reduce', mn[:], ab6[:], AX.X, ALU.min)
        s.op('ve', 'tensor_scalar', w6[:], ab6[:], mn[:], None, ALU.is_le)
        s.op('act', 'activation', w6[:], w6[:], AF.Identity, scale=-2.0, bias=s.cb(1.0, 32))
        s.op('ve', 'tensor_tensor', sgn2[:], sgn[:], w6[:], ALU.mult)
        oh2 = e(nc.sbuf_tensor("oh2", [32, 64], F32))
        pstb = e(nc.psum_tensor("pstb", [64, 64], BF16))
        idbs = e(nc.sbuf_tensor("idbs", [32, 32], BF16))
        s.op('ve', 'tensor_copy', idbs[:], ids[0:32, 0:32])
        for src, dst in ((sgn, oh), (sgn2, oh2)):
            s.op('pe', 'transpose', pst[0:6, 0:32], src[:], ids[0:32, 0:32])
            s.op('act', 'copy', sT[:], pst[0:6, 0:32])
            s.mm(psp[:], sT[:], p64s[:])
            s.op('act', 'activation', dst[:], psp[:], AF.Relu, bias=s.cb(-5.0, 32))
        s.op('ve', 'tensor_tensor', oh[:], oh[:], oh2[:], ALU.add)
        s.op('ve', 'tensor_copy', ohb[:], oh[:])
        s.op('pe', 'transpose', pstb[0:64, 0:32], ohb[:], idbs[:])
        s.op('act', 'copy', ohT[:], pstb[0:64, 0:32])
        for blk in range(128):
            s.dma(wxs[:], wex[:, blk * 8192:(blk + 1) * 8192])
            for ss in range(16):
                s.mm(ps0[0:32, :], ohT[:], wxs[:, ss * 512:(ss + 1) * 512])
                s.op('act', 'copy', wob[:, ss * 512:(ss + 1) * 512], ps0[0:32, :])
            s.dma(wall[:, blk * 8192:(blk + 1) * 8192], wob[:])
        for c in range(NC):
            t0 = c * 128
            for dk in range(8):
                s.dma(wsel[:, dk * 1024:(dk + 1) * 1024],
                      bass.AP(wall.tensor, c * D * D + dk * 128 * 1024, [[1024, 128], [1, 1024]]))
                s.dma(xcb[:, dk * 128:(dk + 1) * 128], xnb[dk * 128:(dk + 1) * 128, t0:t0 + 128])
            for eo in range(8):
                for dk in range(8):
                    s.mm(psy[:], wsel[:, dk * 1024 + eo * 128: dk * 1024 + (eo + 1) * 128],
                         xcb[:, dk * 128:(dk + 1) * 128], start=(dk == 0), stop=(dk == 7))
                s.op('act', 'activation', of[:], psy[:], AF.Identity, scale=0.5)
                s.dma(ff0[eo * 128:(eo + 1) * 128, t0:t0 + 128], of[:])
        s.fin()
    return nc, stc


def colblock_ap(t, rows, cols, c0, cw):
    return bass.AP(t.tensor, c0, [[cols, 128], [128 * cols, rows // 128], [1, cw]])


def v3(sb_ap, k):
    return sb_ap.rearrange("p (k f) -> p k f", k=k)


def b_l1a():
    nc = bass.Bass("TRN2", target_bir_lowering=False)
    stc = St(nc)
    xnb = stc.inp("xn1b", [D, T], BF16)
    xnf = stc.inp("xn1f", [D, T])
    wip = stc.inp("l1_inprojT", [D, 4384], BF16)
    wdt = stc.inp("l1_inprojdtT", [D, 32], F32)
    cw = stc.inp("l1_convw", [2304, 4], F32)
    cb = stc.inp("l1_convb", [2304, 1], F32)
    dtb = stc.inp("l1_dtbias", [32, 1], F32)
    alog = stc.inp("l1_alog", [32, 1], F32)
    drow = stc.inp("l1_drow", [1, 32], F32)
    nw = stc.inp("l1_normw", [128, 16], F32)
    wop = stc.inp("l1_outprojT", [2048, D], BF16)
    o1b = stc.inp("ones1x128b", [1, 128], BF16)
    o1f = stc.inp("ones1x128f", [1, 128], F32)
    idf = stc.inp("id128f", [128, 128], F32)
    ocb = stc.inp("onescol128b", [128, 1], BF16)
    idb = stc.inp("id128b", [128, 128], BF16)
    masks = {sh: stc.inp(f"maskL1n_{sh}", [128, 512], F32) for sh in (0, 128, 256, 384)}
    at1 = stc.out("at1f", [D, T])
    zxb = stc.internal("zxb", [4384, T], BF16)
    dtrf = stc.internal("dtrf", [32, T])
    yzb = stc.internal("yzb", [2048, T], BF16)
    OTS = []
    o = 0
    while o < 4384:
        OTS.append((o, min(128, 4384 - o)))
        o += 128
    with contextlib.ExitStack() as stack:
        e = stack.enter_context
        big = e(nc.sbuf_tensor("big", [128, 36864], BF16))
        xsb = e(nc.sbuf_tensor("xsb", [128, 8 * 512], BF16))
        xsf = e(nc.sbuf_tensor("xsf", [128, 8 * 512], F32))
        dtws = e(nc.sbuf_tensor("dtws", [128, 8 * 32], F32))
        osb = e(nc.sbuf_tensor("osb", [128, 512], BF16))
        osf = e(nc.sbuf_tensor("osf", [128, 512], F32))
        xpad = e(nc.sbuf_tensor("xpad", [128, 2051], BF16))
        cwv = e(nc.sbuf_tensor("cwv", [128, 4], F32))
        cbv = e(nc.sbuf_tensor("cbv", [128, 1], F32))
        acc = e(nc.sbuf_tensor("acc", [128, 2048], F32))
        dtr = e(nc.sbuf_tensor("dtr", [32, 2048], F32))
        dtv = e(nc.sbuf_tensor("dtv", [32, 2048], F32))
        dta = e(nc.sbuf_tensor("dta", [32, 2048], F32))
        gg = e(nc.sbuf_tensor("gg", [32, 2048], F32))
        ghi = e(nc.sbuf_tensor("ghi", [32, 2048], BF16))
        ghm = e(nc.sbuf_tensor("ghm", [32, 2048], BF16))
        glo = e(nc.sbuf_tensor("glo", [32, 2048], BF16))
        gd = e(nc.sbuf_tensor("gd", [64, 2048], F32))
        gtdt = e(nc.sbuf_tensor("gtdt", [128, 16 * 64], F32))
        dcol = e(nc.sbuf_tensor("dcol", [128, 32], F32))
        dtxT = e(nc.sbuf_tensor("dtxT", [128, 16 * 64], BF16))
        growh = e(nc.sbuf_tensor("growh", [1, 2048], BF16))
        growm = e(nc.sbuf_tensor("growm", [1, 2048], BF16))
        growl = e(nc.sbuf_tensor("growl", [1, 2048], BF16))
        mx = e(nc.sbuf_tensor("mx", [128, 512], BF16))
        smb = e(nc.sbuf_tensor("smb", [128, 512], BF16))
        ytmp = e(nc.sbuf_tensor("ytmp", [64, 512], F32))
        zsl = e(nc.sbuf_tensor("zsl", [64, 512], BF16))
        zss = e(nc.sbuf_tensor("zss", [64, 512], BF16))
        sq2 = e(nc.sbuf_tensor("sq2", [128, 512], BF16))
        rrow = e(nc.sbuf_tensor("rrow", [1, 512], F32))
        rrb = e(nc.sbuf_tensor("rrb", [1, 512], BF16))
        mks = {sh: e(nc.sbuf_tensor(f"mk{sh}", [128, 512], F32)) for sh in (0, 128, 256, 384)}
        mxf = e(nc.sbuf_tensor("mxf", [128, 512], F32))
        ids = e(nc.sbuf_tensor("ids", [128, 128], BF16))
        idsf = e(nc.sbuf_tensor("idsf", [128, 128], F32))
        onesb = e(nc.sbuf_tensor("onesb", [1, 128], BF16))
        onesf = e(nc.sbuf_tensor("onesf", [1, 128], F32))
        occ = e(nc.sbuf_tensor("occ", [128, 1], BF16))
        nws = e(nc.sbuf_tensor("nws", [128, 16], F32))
        dtbs = e(nc.sbuf_tensor("dtbs", [32, 1], F32))
        alogs = e(nc.sbuf_tensor("alogs", [32, 1], F32))
        drows = e(nc.sbuf_tensor("drows", [1, 32], F32))
        ps0 = e(nc.psum_tensor("ps0", [128, 512], F32))
        psg32 = e(nc.psum_tensor("psg32", [32, 512], F32))
        psG = e(nc.psum_tensor("psG", [128, 512], F32))
        psS = e(nc.psum_tensor("psS", [128, 512], F32))
        psY = e(nc.psum_tensor("psY", [64, 512], F32))
        psTb = e(nc.psum_tensor("psTb", [128, 64], BF16))
        psR = e(nc.psum_tensor("psR", [1, 512], F32))
        psRB = e(nc.psum_tensor("psRB", [128, 512], F32))
        s = Seq(nc, stack)
        s.dma(ids[:], idb)
        s.dma(idsf[:], idf)
        s.dma(onesb[:], o1b)
        s.dma(onesf[:], o1f)
        s.dma(occ[:], ocb)
        s.dma(nws[:], nw)
        s.dma(dtbs[:], dtb)
        s.dma(alogs[:], alog)
        s.dma(drows[:], drow)
        for sh in (0, 128, 256, 384):
            s.dma(mks[sh][:], masks[sh])
        s.mm(ps0[:, 0:32], onesf[:], drows[:])
        s.op('act', 'copy', dcol[:], ps0[:, 0:32])
        for kt in range(8):
            s.dma(big[:, kt * 4384:(kt + 1) * 4384], wip[kt * 128:(kt + 1) * 128, :])
            s.dma(dtws[:, kt * 32:(kt + 1) * 32], wdt[kt * 128:(kt + 1) * 128, :])
        for ts in range(8):
            c0 = ts * 512
            s.dma(v3(xsb[:], 8), colblock_ap(xnb, D, T, c0, 512))
            s.dma(v3(xsf[:], 8), colblock_ap(xnf, D, T, c0, 512))
            for (o0, mo) in OTS:
                for kt in range(8):
                    s.mm(ps0[0:mo, :], big[:, kt * 4384 + o0: kt * 4384 + o0 + mo],
                         xsb[:, kt * 512:(kt + 1) * 512], start=(kt == 0), stop=(kt == 7))
                s.op('act', 'copy', osb[0:mo, :], ps0[0:mo, :])
                s.dma(zxb[o0:o0 + mo, c0:c0 + 512], osb[0:mo, :])
            for kt in range(8):
                s.mm(psg32[:], dtws[:, kt * 32:(kt + 1) * 32],
                     xsf[:, kt * 512:(kt + 1) * 512], start=(kt == 0), stop=(kt == 7))
            s.op('act', 'copy', dtr[:, 0:512], psg32[:])
            s.dma(dtrf[:, c0:c0 + 512], dtr[:, 0:512])
        for b in range(B_PER):
            cbase = b * S
            for ct in range(18):
                r0 = 2048 + ct * 128
                s.op('ve', 'memset', xpad[:, 0:3], 0.0)
                s.dma(xpad[:, 3:2051], zxb[r0:r0 + 128, cbase:cbase + S])
                s.dma(cwv[:], cw[ct * 128:(ct + 1) * 128, :])
                s.dma(cbv[:], cb[ct * 128:(ct + 1) * 128, :])
                s.op('ve', 'tensor_scalar', acc[:], xpad[:, 0:2048], cwv[:, 0:1], None, ALU.mult)
                for k in (1, 2, 3):
                    s.op('ve', 'scalar_tensor_tensor', acc[:], xpad[:, k:k + 2048],
                         cwv[:, k:k + 1], acc[:], ALU.mult, ALU.add)
                s.op('act', 'activation', big[:, ct * 2048:(ct + 1) * 2048], acc[:],
                     AF.Silu, bias=cbv[:])
            s.dma(dtr[:], dtrf[:, cbase:cbase + S])
            s.op('act', 'activation', dtr[:], dtr[:], AF.Identity, bias=dtbs[:])
            softplus_ops(s, dtv[:], dtr[:], acc[0:32, :], gd[0:32, :], p=32)
            s.op('act', 'activation', gd[0:32, 0:1], alogs[:], AF.Exp)
            s.op('ve', 'tensor_scalar', dta[:], dtv[:], gd[0:32, 0:1], -1.0, ALU.mult, ALU.mult)
            s.op('ve', 'memset', acc[0:32, :], 0.0)
            s.op('ve', 'tensor_tensor_scan', gg[:], dta[:], acc[0:32, :], 0.0, ALU.add, ALU.add)
            s.op('ve', 'tensor_copy', ghi[:], gg[:])
            s.op('ve', 'tensor_copy', dtr[:], ghi[:])
            s.op('ve', 'tensor_tensor', acc[0:32, :], gg[:], dtr[:], ALU.subtract)
            s.op('ve', 'tensor_copy', ghm[:], acc[0:32, :])
            s.op('ve', 'tensor_copy', dta[:], ghm[:])
            s.op('ve', 'tensor_tensor', acc[0:32, :], acc[0:32, :], dta[:], ALU.subtract)
            s.op('ve', 'tensor_copy', glo[:], acc[0:32, :])
            s.op('ve', 'tensor_copy', gd[0:32, :], gg[:])
            s.op('ve', 'tensor_copy', gd[32:64, :], dtv[:])
            for tt in range(16):
                s.op('pe', 'transpose', ps0[:, 0:64], gd[:, tt * 128:(tt + 1) * 128],
                     idsf[0:64, 0:64])
                s.op('act', 'activation', gtdt[:, tt * 64: tt * 64 + 32], ps0[:, 0:32],
                     AF.Identity, scale=-1.0)
                s.op('act', 'copy', gtdt[:, tt * 64 + 32: tt * 64 + 64], ps0[:, 32:64])
            for h in range(32):
                ct = h // 2
                rr0 = (h % 2) * 64
                xsm0 = ct * 2048
                for tt in range(16):
                    s.op('pe', 'transpose', psTb[:, 0:64],
                         big[rr0:rr0 + 64, xsm0 + tt * 128: xsm0 + (tt + 1) * 128],
                         ids[rr0:rr0 + 64, rr0:rr0 + 64])
                    s.op('ve', 'tensor_scalar', dtxT[:, tt * 64:(tt + 1) * 64], psTb[:, 0:64],
                         gtdt[:, tt * 64 + 32 + h: tt * 64 + 33 + h], None, ALU.mult)
                s.dma(growh[:], ghi[h:h + 1, :])
                s.dma(growm[:], ghm[h:h + 1, :])
                s.dma(growl[:], glo[h:h + 1, :])
                for tqs in range(4):
                    q0 = tqs * 512
                    s.mm(psG[:], onesb[:], growh[0:1, q0:q0 + 512], start=True, stop=False)
                    s.mm(psG[:], onesb[:], growm[0:1, q0:q0 + 512], start=False, stop=False)
                    s.mm(psG[:], onesb[:], growl[0:1, q0:q0 + 512], start=False, stop=True)
                    nts = 4 * (tqs + 1)
                    for ts in range(nts):
                        s.mm(psS[:], big[:, 16 * 2048 + ts * 128: 16 * 2048 + (ts + 1) * 128],
                             big[:, 17 * 2048 + q0: 17 * 2048 + q0 + 512])
                        sh = ts * 128 - q0
                        if sh in (0, 128, 256, 384):
                            s.op('act', 'activation', mxf[:], psG[:], AF.Identity,
                                 bias=gtdt[:, ts * 64 + h:ts * 64 + h + 1])
                            s.op('ve', 'tensor_tensor', mxf[:], mxf[:], mks[sh][:], ALU.add)
                            s.op('act', 'activation', mx[:], mxf[:], AF.Exp)
                        else:
                            s.op('act', 'activation', mx[:], psG[:], AF.Exp,
                                 bias=gtdt[:, ts * 64 + h:ts * 64 + h + 1])
                        s.op('ve', 'tensor_tensor', smb[:], psS[:], mx[:], ALU.mult)
                        s.mm(psY[:], dtxT[:, ts * 64:(ts + 1) * 64], smb[:],
                             start=(ts == 0), stop=(ts == nts - 1))
                    s.op('ve', 'scalar_tensor_tensor', ytmp[:],
                         big[rr0:rr0 + 64, xsm0 + q0: xsm0 + q0 + 512],
                         dcol[rr0:rr0 + 64, h:h + 1], psY[:], ALU.mult, ALU.add)
                    s.dma(zsl[:], zxb[h * 64:(h + 1) * 64, cbase + q0: cbase + q0 + 512])
                    s.op('act', 'activation', zss[:], zsl[:], AF.Silu)
                    s.op('ve', 'tensor_tensor', zss[:], ytmp[:], zss[:], ALU.mult)
                    s.dma(yzb[h * 64:(h + 1) * 64, cbase + q0: cbase + q0 + 512], zss[:])
            for kt in range(16):
                s.dma(big[:, kt * 1024:(kt + 1) * 1024], wop[kt * 128:(kt + 1) * 128, :])
            for ts4 in range(4):
                c0 = cbase + ts4 * 512
                s.dma(v3(big[:, 24576:32768], 16), colblock_ap(yzb, 2048, T, c0, 512))
                for kt in range(16):
                    sl = big[:, 24576 + kt * 512: 24576 + (kt + 1) * 512]
                    s.op('ve', 'tensor_tensor', sq2[:], sl, sl, ALU.mult)
                    s.mm(psR[:], occ[:], sq2[:], start=(kt == 0), stop=(kt == 15))
                s.op('act', 'mul', rrow[:], psR[:], 1.0 / 2048.0)
                s.op('act', 'activation', rrow[:], rrow[:], AF.Ln, bias=s.cb(1e-5, 1))
                s.op('act', 'activation', rrow[:], rrow[:], AF.Exp, scale=-0.5)
                s.op('ve', 'tensor_copy', rrb[:], rrow[:])
                s.mm(psRB[:], onesb[:], rrb[:])
                for kt in range(16):
                    s.op('ve', 'scalar_tensor_tensor',
                         big[:, 16384 + kt * 512: 16384 + (kt + 1) * 512],
                         big[:, 24576 + kt * 512: 24576 + (kt + 1) * 512],
                         nws[:, kt:kt + 1], psRB[:], ALU.mult, ALU.mult)
                for ot in range(8):
                    for kt in range(16):
                        s.mm(ps0[:], big[:, kt * 1024 + ot * 128: kt * 1024 + (ot + 1) * 128],
                             big[:, 16384 + kt * 512: 16384 + (kt + 1) * 512],
                             start=(kt == 0), stop=(kt == 15))
                    s.op('act', 'copy', osf[:], ps0[:])
                    s.dma(at1[ot * 128:(ot + 1) * 128, c0:c0 + 512], osf[:])
        s.fin()
    return nc, stc


def b_l1f():
    nc = bass.Bass("TRN2", target_bir_lowering=False)
    stc = St(nc)
    xnb = stc.inp("xn1b", [D, T], BF16)
    w1 = stc.inp("l1f_w1T", [D, 4096], BF16)
    w2 = stc.inp("l1f_w2T", [D, 4096], BF16)
    h1 = stc.inp("l1f_h1T", [4096, 4096], BF16)
    h2 = stc.inp("l1f_h2T", [4096, 4096], BF16)
    w3 = stc.inp("l1f_w3T", [4096, D], BF16)
    s1 = stc.inp("l1f_s1", [128, 32], F32)
    s2 = stc.inp("l1f_s2", [128, 32], F32)
    sh1 = stc.inp("l1f_sh1", [128, 32], F32)
    sh2 = stc.inp("l1f_sh2", [128, 32], F32)
    s3 = stc.inp("l1f_s3", [128, 8], F32)
    ff1 = stc.out("ff1f", [D, T])
    hb = stc.internal("hb1", [4096, T], BF16)
    gb = stc.internal("gb1", [4096, T], BF16)
    g2b = stc.internal("g2b1", [4096, T], BF16)
    with contextlib.ExitStack() as stack:
        e = stack.enter_context
        xall = e(nc.sbuf_tensor("xall", [128, 8 * T], BF16))
        wc1 = e(nc.sbuf_tensor("wc1", [128, 8 * 128], BF16))
        wc2 = e(nc.sbuf_tensor("wc2", [128, 8 * 128], BF16))
        wch = e(nc.sbuf_tensor("wch", [128, 32 * 128], BF16))
        hcol = e(nc.sbuf_tensor("hcol", [128, 32 * 512], BF16))
        t1 = e(nc.sbuf_tensor("t1", [128, 512], BF16))
        t2 = e(nc.sbuf_tensor("t2", [128, 512], BF16))
        osf = e(nc.sbuf_tensor("osf", [128, 512], F32))
        s1s = e(nc.sbuf_tensor("s1s", [128, 32], F32))
        s2s = e(nc.sbuf_tensor("s2s", [128, 32], F32))
        sh1s = e(nc.sbuf_tensor("sh1s", [128, 32], F32))
        sh2s = e(nc.sbuf_tensor("sh2s", [128, 32], F32))
        s3s = e(nc.sbuf_tensor("s3s", [128, 8], F32))
        ps1 = e(nc.psum_tensor("ps1", [128, 512], F32))
        ps2 = e(nc.psum_tensor("ps2", [128, 512], F32))
        s = Seq(nc, stack)
        for nm, dst in ((s1, s1s), (s2, s2s), (sh1, sh1s), (sh2, sh2s), (s3, s3s)):
            s.dma(dst[:], nm)
        s.dma(v3(xall[:], 8), colblock_ap(xnb, D, T, 0, T))
        for ot in range(32):
            s.dma(v3(wc1[:], 8), colblock_ap(w1, D, 4096, ot * 128, 128))
            s.dma(v3(wc2[:], 8), colblock_ap(w2, D, 4096, ot * 128, 128))
            for ts in range(8):
                c0 = ts * 512
                for kt in range(8):
                    s.mm(ps1[:], wc1[:, kt * 128:(kt + 1) * 128],
                         xall[:, kt * T + c0: kt * T + c0 + 512], start=(kt == 0), stop=(kt == 7))
                for kt in range(8):
                    s.mm(ps2[:], wc2[:, kt * 128:(kt + 1) * 128],
                         xall[:, kt * T + c0: kt * T + c0 + 512], start=(kt == 0), stop=(kt == 7))
                s.op('act', 'activation', t1[:], ps1[:], AF.Silu, scale=s1s[:, ot:ot + 1])
                s.op('act', 'activation', t2[:], ps2[:], AF.Identity, scale=s2s[:, ot:ot + 1])
                s.op('ve', 'tensor_tensor', t1[:], t1[:], t2[:], ALU.mult)
                s.dma(hb[ot * 128:(ot + 1) * 128, c0:c0 + 512], t1[:])
        for (src, wmat, ssc, dst) in ((hb, h1, sh1s, gb), (gb, h2, sh2s, g2b)):
            for ts in range(8):
                c0 = ts * 512
                s.dma(v3(hcol[:], 32), colblock_ap(src, 4096, T, c0, 512))
                for ot in range(32):
                    s.dma(v3(wch[:], 32), colblock_ap(wmat, 4096, 4096, ot * 128, 128))
                    for kt in range(32):
                        s.mm(ps1[:], wch[:, kt * 128:(kt + 1) * 128],
                             hcol[:, kt * 512:(kt + 1) * 512], start=(kt == 0), stop=(kt == 31))
                    s.op('act', 'activation', t1[:], ps1[:], AF.Gelu, scale=ssc[:, ot:ot + 1])
                    s.dma(dst[ot * 128:(ot + 1) * 128, c0:c0 + 512], t1[:])
        for ts in range(8):
            c0 = ts * 512
            s.dma(v3(hcol[:], 32), colblock_ap(g2b, 4096, T, c0, 512))
            for ot in range(8):
                s.dma(v3(wch[:, 0:32 * 128], 32), colblock_ap(w3, 4096, D, ot * 128, 128))
                for kt in range(32):
                    s.mm(ps1[:], wch[:, kt * 128:(kt + 1) * 128],
                         hcol[:, kt * 512:(kt + 1) * 512], start=(kt == 0), stop=(kt == 31))
                s.op('act', 'activation', osf[:], ps1[:], AF.Identity, scale=s3s[:, ot:ot + 1])
                s.dma(ff1[ot * 128:(ot + 1) * 128, c0:c0 + 512], osf[:])
        s.fin()
    return nc, stc


def b_l2p():
    nc = bass.Bass("TRN2", target_bir_lowering=False)
    stc = St(nc)
    xnf = stc.inp("xn2f", [D, T])
    mixes = {p: stc.inp(f"l2_mx_{p}", [D, 1], F32) for p in "rwkvag"}
    Wr = stc.inp("l2_Wr", [D, D], BF16)
    Wk = stc.inp("l2_Wk", [D, D], BF16)
    Wv = stc.inp("l2_Wv", [D, D], BF16)
    w0 = stc.inp("l2_w0", [D, 1], F32)
    w1 = stc.inp("l2_w1", [D, 64], BF16)
    w2 = stc.inp("l2_w2", [64, D], BF16)
    a0 = stc.inp("l2_a0", [D, 1], F32)
    a1 = stc.inp("l2_a1", [D, 64], BF16)
    a2 = stc.inp("l2_a2", [64, D], BF16)
    g1 = stc.inp("l2_g1", [D, 128], BF16)
    g2 = stc.inp("l2_g2", [128, D], BF16)
    kkc = stc.inp("l2_kk", [D, 1], F32)
    kac = stc.inp("l2_ka", [D, 1], F32)
    rkc = stc.inp("l2_rk", [D, 1], F32)
    blkb = stc.inp("blk64b", [128, 2], BF16)
    blkT = stc.inp("blk64Tb", [2, 128], BF16)
    idb = stc.inp("id128b", [128, 128], BF16)
    strr = stc.out("strr", [128, 16 * S], BF16)
    strw = stc.out("strw", [128, 16 * S], F32)
    strk = stc.out("strk", [128, 16 * S], BF16)
    stra = stc.out("stra", [128, 16 * S], BF16)
    strb = stc.out("strb", [128, 16 * S], BF16)
    strv = stc.out("strv", [2, S * D], BF16)
    gout = stc.out("gout", [D, T], BF16)
    rwkf = stc.out("rwkf", [D, T])
    xrb = stc.internal("xrb", [D, T], BF16)
    xwb = stc.internal("xwb", [D, T], BF16)
    xkb = stc.internal("xkb", [D, T], BF16)
    xvb = stc.internal("xvb", [D, T], BF16)
    xab = stc.internal("xab", [D, T], BF16)
    xgb = stc.internal("xgb", [D, T], BF16)
    rfm = stc.internal("rfm", [D, T], BF16)
    kfm = stc.internal("kfm", [D, T], BF16)
    vfm = stc.internal("vfm", [D, T], BF16)
    aafm = stc.internal("aafm", [D, T], BF16)
    wfm = stc.internal("wfm", [D, T])
    kknfm = stc.internal("kknfm", [D, T], BF16)
    knfm = stc.internal("knfm", [D, T], BF16)
    bfm = stc.internal("bfm", [D, T], BF16)
    mixv = dict(r=xrb, w=xwb, k=xkb, v=xvb, a=xab, g=xgb)
    with contextlib.ExitStack() as stack:
        e = stack.enter_context
        xbt = e(nc.sbuf_tensor("xbt", [128, S], F32))
        sh = e(nc.sbuf_tensor("sh", [128, S], F32))
        xx = e(nc.sbuf_tensor("xx", [128, S], F32))
        mxt = e(nc.sbuf_tensor("mxt", [128, S], BF16))
        mcols = e(nc.sbuf_tensor("mcols", [128, 64], F32))
        wsb = e(nc.sbuf_tensor("wsb", [128, 8 * 1024], BF16))
        xsb = e(nc.sbuf_tensor("xsb", [128, 8 * 512], BF16))
        osb = e(nc.sbuf_tensor("osb", [128, 512], BF16))
        lsml = e(nc.sbuf_tensor("lsml", [128, 8 * 64], BF16))
        lor = e(nc.sbuf_tensor("lor", [64, T], BF16))
        glr = e(nc.sbuf_tensor("glr", [128, T], BF16))
        lg2 = e(nc.sbuf_tensor("lg2", [128, 8 * 128], BF16))
        wr0 = e(nc.sbuf_tensor("wr0", [128, 512], F32))
        tb1 = e(nc.sbuf_tensor("tb1", [128, 512], F32))
        tb2 = e(nc.sbuf_tensor("tb2", [128, 512], F32))
        tb3 = e(nc.sbuf_tensor("tb3", [128, 512], F32))
        ccols = e(nc.sbuf_tensor("ccols", [128, 8 * 8], F32))
        ktile = e(nc.sbuf_tensor("ktile", [128, 512], BF16))
        aat = e(nc.sbuf_tensor("aat", [128, 512], BF16))
        rtile = e(nc.sbuf_tensor("rtile", [128, 512], BF16))
        vtile = e(nc.sbuf_tensor("vtile", [128, 512], BF16))
        kkp = e(nc.sbuf_tensor("kkp", [128, 512], F32))
        sqb = e(nc.sbuf_tensor("sqb", [128, 512], BF16))
        nrm = e(nc.sbuf_tensor("nrm", [2, 512], F32))
        nrb = e(nc.sbuf_tensor("nrb", [2, 512], BF16))
        kkn = e(nc.sbuf_tensor("kkn", [128, 512], BF16))
        blks = e(nc.sbuf_tensor("blks", [128, 2], BF16))
        blkTs = e(nc.sbuf_tensor("blkTs", [2, 128], BF16))
        ids = e(nc.sbuf_tensor("ids", [128, 128], BF16))
        psw = e(nc.psum_tensor("psw", [128, 512], F32))
        psl = e(nc.psum_tensor("psl", [64, 512], F32))
        psn = e(nc.psum_tensor("psn", [2, 512], F32))
        psb = e(nc.psum_tensor("psb", [128, 512], F32))
        psTb = e(nc.psum_tensor("psTb", [128, 128], BF16))
        s = Seq(nc, stack)
        s.dma(blks[:], blkb)
        s.dma(blkTs[:], blkT)
        s.dma(ids[:], idb)
        for i, p in enumerate("rwkvag"):
            for dt in range(8):
                s.dma(mcols[:, i * 8 + dt:i * 8 + dt + 1],
                      mixes[p][dt * 128:(dt + 1) * 128, :])
        for b in range(B_PER):
            cb = b * S
            for dt in range(8):
                s.dma(xbt[:], xnf[dt * 128:(dt + 1) * 128, cb:cb + S])
                s.op('ve', 'memset', sh[:, 0:1], 0.0)
                s.op('ve', 'tensor_copy', sh[:, 1:S], xbt[:, 0:S - 1])
                s.op('ve', 'tensor_tensor', xx[:], sh[:], xbt[:], ALU.subtract)
                for i, p in enumerate("rwkvag"):
                    s.op('ve', 'scalar_tensor_tensor', mxt[:], xx[:],
                         mcols[:, i * 8 + dt:i * 8 + dt + 1], xbt[:], ALU.mult, ALU.add)
                    s.dma(mixv[p][dt * 128:(dt + 1) * 128, cb:cb + S], mxt[:])
        for wt, xin, dst in ((Wr, xrb, rfm), (Wk, xkb, kfm), (Wv, xvb, vfm)):
            for kt in range(8):
                s.dma(wsb[:, kt * 1024:(kt + 1) * 1024], wt[kt * 128:(kt + 1) * 128, :])
            for ts in range(8):
                c0 = ts * 512
                s.dma(v3(xsb[:], 8), colblock_ap(xin, D, T, c0, 512))
                for ot in range(8):
                    for kt in range(8):
                        s.mm(psw[:], wsb[:, kt * 1024 + ot * 128: kt * 1024 + (ot + 1) * 128],
                             xsb[:, kt * 512:(kt + 1) * 512], start=(kt == 0), stop=(kt == 7))
                    s.op('act', 'copy', osb[:], psw[:])
                    s.dma(dst[ot * 128:(ot + 1) * 128, c0:c0 + 512], osb[:])
        for kt in range(8):
            s.dma(lsml[:, kt * 64:(kt + 1) * 64], w1[kt * 128:(kt + 1) * 128, :])
        for ts in range(8):
            c0 = ts * 512
            s.dma(v3(xsb[:], 8), colblock_ap(xwb, D, T, c0, 512))
            for kt in range(8):
                s.mm(psl[:], lsml[:, kt * 64:(kt + 1) * 64], xsb[:, kt * 512:(kt + 1) * 512],
                     start=(kt == 0), stop=(kt == 7))
            s.op('act', 'activation', lor[:, c0:c0 + 512], psl[:], AF.Tanh)
        for dt in range(8):
            s.dma(mcols[:, 48 + dt:48 + dt + 1], w0[dt * 128:(dt + 1) * 128, :])
        s.dma(wsb[0:64, 0:1024], w2[0:64, 0:1024])
        for ts in range(8):
            c0 = ts * 512
            for ot in range(8):
                s.mm(psw[:], wsb[0:64, ot * 128:(ot + 1) * 128], lor[:, c0:c0 + 512])
                s.op('act', 'activation', wr0[:], psw[:], AF.Identity,
                     bias=mcols[:, 48 + ot:48 + ot + 1])
                s.op('act', 'activation', tb3[:], wr0[:], AF.Identity, scale=-1.0)
                softplus_ops(s, tb1[:], tb3[:], tb2[:], tb3[:])
                s.op('act', 'activation', tb1[:], tb1[:], AF.Identity, scale=-1.0, bias=s.cb(-0.5, 128))
                s.op('act', 'activation', tb1[:], tb1[:], AF.Exp)
                s.op('act', 'activation', tb1[:], tb1[:], AF.Exp, scale=-1.0)
                s.dma(wfm[ot * 128:(ot + 1) * 128, c0:c0 + 512], tb1[:])
        for kt in range(8):
            s.dma(lsml[:, kt * 64:(kt + 1) * 64], a1[kt * 128:(kt + 1) * 128, :])
        for ts in range(8):
            c0 = ts * 512
            s.dma(v3(xsb[:], 8), colblock_ap(xab, D, T, c0, 512))
            for kt in range(8):
                s.mm(psl[:], lsml[:, kt * 64:(kt + 1) * 64], xsb[:, kt * 512:(kt + 1) * 512],
                     start=(kt == 0), stop=(kt == 7))
            s.op('act', 'copy', lor[:, c0:c0 + 512], psl[:])
        for dt in range(8):
            s.dma(mcols[:, 48 + dt:48 + dt + 1], a0[dt * 128:(dt + 1) * 128, :])
        s.dma(wsb[0:64, 0:1024], a2[0:64, 0:1024])
        for ts in range(8):
            c0 = ts * 512
            for ot in range(8):
                s.mm(psw[:], wsb[0:64, ot * 128:(ot + 1) * 128], lor[:, c0:c0 + 512])
                s.op('act', 'activation', osb[:], psw[:], AF.Sigmoid,
                     bias=mcols[:, 48 + ot:48 + ot + 1])
                s.dma(aafm[ot * 128:(ot + 1) * 128, c0:c0 + 512], osb[:])
        for kt in range(8):
            s.dma(lg2[:, kt * 128:(kt + 1) * 128], g1[kt * 128:(kt + 1) * 128, :])
        for ts in range(8):
            c0 = ts * 512
            s.dma(v3(xsb[:], 8), colblock_ap(xgb, D, T, c0, 512))
            for kt in range(8):
                s.mm(psw[:], lg2[:, kt * 128:(kt + 1) * 128], xsb[:, kt * 512:(kt + 1) * 512],
                     start=(kt == 0), stop=(kt == 7))
            s.op('act', 'activation', glr[:, c0:c0 + 512], psw[:], AF.Sigmoid)
        s.dma(wsb[:, 0:1024], g2[:, 0:1024])
        for ts in range(8):
            c0 = ts * 512
            for ot in range(8):
                s.mm(psw[:], wsb[:, ot * 128:(ot + 1) * 128], glr[:, c0:c0 + 512])
                s.op('act', 'copy', osb[:], psw[:])
                s.dma(gout[ot * 128:(ot + 1) * 128, c0:c0 + 512], osb[:])
        for i, cc in enumerate((kkc, kac, rkc)):
            for dt in range(8):
                s.dma(ccols[:, i * 8 + dt:i * 8 + dt + 1], cc[dt * 128:(dt + 1) * 128, :])
        for ts in range(8):
            c0 = ts * 512
            for ot in range(8):
                s.dma(ktile[:], kfm[ot * 128:(ot + 1) * 128, c0:c0 + 512])
                s.dma(aat[:], aafm[ot * 128:(ot + 1) * 128, c0:c0 + 512])
                s.dma(rtile[:], rfm[ot * 128:(ot + 1) * 128, c0:c0 + 512])
                s.dma(vtile[:], vfm[ot * 128:(ot + 1) * 128, c0:c0 + 512])
                s.op('ve', 'tensor_scalar', kkp[:], ktile[:], ccols[:, ot:ot + 1], None, ALU.mult)
                s.op('ve', 'tensor_tensor', sqb[:], kkp[:], kkp[:], ALU.mult)
                s.mm(psn[:], blks[:], sqb[:])
                s.op('act', 'activation', nrm[:], psn[:], AF.Sqrt)
                s.op('ve', 'tensor_scalar', nrm[:], nrm[:], 1e-12, None, ALU.max)
                s.op('ve', 'reciprocal', nrm[:], nrm[:])
                s.op('ve', 'tensor_copy', nrb[:], nrm[:])
                s.mm(psb[:], blkTs[:], nrb[:])
                s.op('ve', 'tensor_tensor', kkn[:], kkp[:], psb[:], ALU.mult)
                s.dma(kknfm[ot * 128:(ot + 1) * 128, c0:c0 + 512], kkn[:])
                s.op('ve', 'tensor_tensor', sqb[:], kkn[:], aat[:], ALU.mult)
                s.dma(bfm[ot * 128:(ot + 1) * 128, c0:c0 + 512], sqb[:])
                s.op('ve', 'tensor_scalar', tb1[:], aat[:], 1.0, ccols[:, 8 + ot:8 + ot + 1],
                     ALU.subtract, ALU.mult)
                s.op('ve', 'tensor_tensor', tb2[:], ktile[:], tb1[:], ALU.mult)
                s.op('ve', 'tensor_tensor', kkn[:], ktile[:], tb2[:], ALU.add)
                s.dma(knfm[ot * 128:(ot + 1) * 128, c0:c0 + 512], kkn[:])
                s.op('ve', 'tensor_tensor', tb1[:], rtile[:], kkn[:], ALU.mult)
                s.op('ve', 'tensor_scalar', tb1[:], tb1[:], ccols[:, 16 + ot:16 + ot + 1],
                     None, ALU.mult)
                s.op('ve', 'tensor_copy', sqb[:], tb1[:])
                s.mm(psn[:], blks[:], sqb[:])
                s.op('act', 'copy', nrb[:], psn[:])
                s.mm(psb[:], blkTs[:], nrb[:])
                s.op('ve', 'tensor_tensor', tb2[:], psb[:], vtile[:], ALU.mult)
                s.dma(rwkf[ot * 128:(ot + 1) * 128, c0:c0 + 512], tb2[:])
        for (src, dst) in ((rfm, strr), (wfm, strw), (knfm, strk), (kknfm, stra), (bfm, strb)):
            for b in range(B_PER):
                for h in range(16):
                    s.dma(bass.AP(dst.tensor, (b * 64) * (16 * S) + h * S, [[16 * S, 64], [1, S]]),
                          src[h * 64:(h + 1) * 64, b * S:(b + 1) * S])
        for b in range(B_PER):
            for ct in range(8):
                for tt in range(16):
                    s.dma(mxt[:, 0:128],
                          vfm[ct * 128:(ct + 1) * 128, b * S + tt * 128: b * S + (tt + 1) * 128])
                    s.op('pe', 'transpose', psTb[:], mxt[:, 0:128], ids[:])
                    s.op('act', 'copy', osb[:, 0:128], psTb[:])
                    s.dma(bass.AP(strv.tensor, b * S * D + (tt * 128) * D + ct * 128,
                                  [[D, 128], [1, 128]]), osb[:, 0:128])
        s.fin()
    return nc, stc


def b_l2s():
    nc = bass.Bass("TRN2", target_bir_lowering=False)
    stc = St(nc)
    SEG = 512
    CH = 16
    sr = stc.inp("seg_r", [128, 16 * SEG], BF16)
    sw = stc.inp("seg_w", [128, 16 * SEG], F32)
    sk = stc.inp("seg_k", [128, 16 * SEG], BF16)
    sa = stc.inp("seg_a", [128, 16 * SEG], BF16)
    sb_ = stc.inp("seg_b", [128, 16 * SEG], BF16)
    sv = stc.inp("seg_v", [2, SEG * D], BF16)
    sin = stc.inp("seg_sin", [128, D], F32)
    blkb = stc.inp("blk64b", [128, 2], BF16)
    blkT = stc.inp("blk64Tb", [2, 128], BF16)
    sout = stc.out("seg_sout", [128, D])
    oseg = stc.out("seg_o", [2, SEG * D], BF16)
    with contextlib.ExitStack() as stack:
        e = stack.enter_context
        rb = e(nc.sbuf_tensor("rb", [128, 16 * SEG], BF16))
        wb = e(nc.sbuf_tensor("wb", [128, 16 * SEG], F32))
        kb = e(nc.sbuf_tensor("kb", [128, 16 * SEG], BF16))
        ab = e(nc.sbuf_tensor("ab", [128, 16 * SEG], BF16))
        bb = e(nc.sbuf_tensor("bb", [128, 16 * SEG], BF16))
        vb = e(nc.sbuf_tensor("vb", [2, CH * D], BF16))
        orng = e(nc.sbuf_tensor("orng", [2, CH * D], BF16))
        SS = e(nc.sbuf_tensor("SS", [128, D], F32))
        tmpb = e(nc.sbuf_tensor("tmpb", [128, D], BF16))
        sasb = e(nc.sbuf_tensor("sasb", [2, D], BF16))
        t4 = e(nc.sbuf_tensor("t4", [128, D], F32))
        blks = e(nc.sbuf_tensor("blks", [128, 2], BF16))
        blkTs = e(nc.sbuf_tensor("blkTs", [2, 128], BF16))
        psA0 = e(nc.psum_tensor("psA0", [2, 512], F32))
        psA1 = e(nc.psum_tensor("psA1", [2, 512], F32))
        psB0 = e(nc.psum_tensor("psB0", [128, 512], F32))
        psB1 = e(nc.psum_tensor("psB1", [128, 512], F32))
        psV0 = e(nc.psum_tensor("psV0", [128, 512], F32))
        psV1 = e(nc.psum_tensor("psV1", [128, 512], F32))
        psO0 = e(nc.psum_tensor("psO0", [2, 512], F32))
        psO1 = e(nc.psum_tensor("psO1", [2, 512], F32))
        s = Seq(nc, stack)
        s.dma(blks[:], blkb)
        s.dma(blkTs[:], blkT)
        s.dma(rb[:], sr)
        s.dma(wb[:], sw)
        s.dma(kb[:], sk)
        s.dma(ab[:], sa)
        s.dma(bb[:], sb_)
        s.dma(SS[:], sin)
        r3 = rb[:].rearrange("p (h t) -> p h t", h=16)
        w3 = wb[:].rearrange("p (h t) -> p h t", h=16)
        k3 = kb[:].rearrange("p (h t) -> p h t", h=16)
        a3 = ab[:].rearrange("p (h t) -> p h t", h=16)
        b3 = bb[:].rearrange("p (h t) -> p h t", h=16)
        Sv = SS[:].rearrange("p (h i) -> p h i", h=16)
        t4v = t4[:].rearrange("p (h i) -> p h i", h=16)
        tmpv = tmpb[:].rearrange("p (h i) -> p h i", h=16)
        for ch in range(SEG // CH):
            s.dma(vb[:], bass.AP(sv.tensor, ch * CH * D, [[SEG * D, 2], [1, CH * D]]))
            for tl in range(CH):
                t = ch * CH + tl
                avw = a3[:, :, t:t + 1].broadcast_to([128, 16, 64])
                wvw = w3[:, :, t:t + 1].broadcast_to([128, 16, 64])
                kvw = k3[:, :, t:t + 1].broadcast_to([128, 16, 64])
                bvw = b3[:, :, t:t + 1].broadcast_to([128, 16, 64])
                rvw = r3[:, :, t:t + 1].broadcast_to([128, 16, 64])
                s.op('ve', 'tensor_tensor', tmpv, Sv, avw, ALU.mult)
                s.mm(psA0[:], blks[:], tmpb[:, 0:512])
                s.mm(psA1[:], blks[:], tmpb[:, 512:1024])
                s.op('act', 'activation', sasb[:, 0:512], psA0[:], AF.Identity, scale=-1.0)
                s.op('act', 'activation', sasb[:, 512:1024], psA1[:], AF.Identity, scale=-1.0)
                s.mm(psB0[:], blkTs[:], sasb[:, 0:512])
                s.mm(psB1[:], blkTs[:], sasb[:, 512:1024])
                s.mm(psV0[:], blkTs[:], vb[:, tl * D: tl * D + 512])
                s.mm(psV1[:], blkTs[:], vb[:, tl * D + 512: (tl + 1) * D])
                s.op('ve', 'tensor_tensor', Sv, Sv, wvw, ALU.mult)
                s.op('ve', 'tensor_tensor', t4v[:, 0:8, :], psB0[:].rearrange("p (h i) -> p h i", h=8), bvw[:, 0:8, :], ALU.mult)
                s.op('ve', 'tensor_tensor', t4v[:, 8:16, :], psB1[:].rearrange("p (h i) -> p h i", h=8), bvw[:, 8:16, :], ALU.mult)
                s.op('ve', 'tensor_tensor', SS[:], SS[:], t4[:], ALU.add)
                s.op('ve', 'tensor_tensor', t4v[:, 0:8, :], psV0[:].rearrange("p (h i) -> p h i", h=8), kvw[:, 0:8, :], ALU.mult)
                s.op('ve', 'tensor_tensor', t4v[:, 8:16, :], psV1[:].rearrange("p (h i) -> p h i", h=8), kvw[:, 8:16, :], ALU.mult)
                s.op('ve', 'tensor_tensor', SS[:], SS[:], t4[:], ALU.add)
                s.op('ve', 'tensor_tensor', tmpv, Sv, rvw, ALU.mult)
                s.mm(psO0[:], blks[:], tmpb[:, 0:512])
                s.mm(psO1[:], blks[:], tmpb[:, 512:1024])
                s.op('act', 'copy', orng[:, tl * D: tl * D + 512], psO0[:])
                s.op('act', 'copy', orng[:, tl * D + 512: (tl + 1) * D], psO1[:])
            s.dma(bass.AP(oseg.tensor, ch * CH * D, [[SEG * D, 2], [1, CH * D]]), orng[:])
        s.dma(sout, SS[:])
        s.fin()
    return nc, stc


def b_l2q():
    nc = bass.Bass("TRN2", target_bir_lowering=False)
    stc = St(nc)
    og = stc.inp("oall", [2, S * D], BF16)
    gy = stc.inp("gout", [D, T], BF16)
    rwk = stc.inp("rwkf", [D, T])
    gam = stc.inp("l2_gam", [128, 8], F32)
    bet = stc.inp("l2_bet", [128, 8], F32)
    Wo = stc.inp("l2_Wo", [D, D], BF16)
    idf = stc.inp("id128f", [128, 128], F32)
    xnb = stc.inp("xn2b", [D, T], BF16)
    win = stc.inp("l2f_win", [D, 64], BF16)
    binc = stc.inp("l2f_bin", [64, 1], F32)
    W1 = stc.inp("l2f_W1", [64, 16], F32)
    b1 = stc.inp("l2f_b1", [64, 16], F32)
    W2 = stc.inp("l2f_W2", [64, 16], F32)
    b2 = stc.inp("l2f_b2", [64, 1], F32)
    wout = stc.inp("l2f_wout", [64, D], BF16)
    bout = stc.inp("l2f_bout", [128, 8], F32)
    at2 = stc.out("at2f", [D, T])
    ff2 = stc.out("ff2f", [D, T])
    with contextlib.ExitStack() as stack:
        e = stack.enter_context
        ybt = e(nc.sbuf_tensor("ybt", [128, 1024], BF16))
        mu = e(nc.sbuf_tensor("mu", [128, 16], F32))
        msq = e(nc.sbuf_tensor("msq", [128, 16], F32))
        mu2 = e(nc.sbuf_tensor("mu2", [128, 16], F32))
        rstd = e(nc.sbuf_tensor("rstd", [128, 16], F32))
        sqf = e(nc.sbuf_tensor("sqf", [128, 1024], F32))
        ynf = e(nc.sbuf_tensor("ynf", [128, 1024], F32))
        tmpf = e(nc.sbuf_tensor("tmpf", [128, 128], F32))
        gms = e(nc.sbuf_tensor("gms", [128, 8], F32))
        bts = e(nc.sbuf_tensor("bts", [128, 8], F32))
        ygfm = e(nc.sbuf_tensor("ygfm", [128, 8 * 2048], BF16))
        idsf = e(nc.sbuf_tensor("idsf", [128, 128], F32))
        wsb = e(nc.sbuf_tensor("wsb", [128, 8 * 1024], BF16))
        osf = e(nc.sbuf_tensor("osf", [128, 512], F32))
        psT = e(nc.psum_tensor("psT", [128, 128], F32))
        ps0 = e(nc.psum_tensor("ps0", [128, 512], F32))
        s = Seq(nc, stack)
        s.dma(idsf[:], idf)
        s.dma(gms[:], gam)
        s.dma(bts[:], bet)
        yv = ybt[:].rearrange("p (h i) -> p h i", h=16)
        sqv = sqf[:].rearrange("p (h i) -> p h i", h=16)
        for b in range(B_PER):
            for tt in range(16):
                s.dma(ybt[:], bass.AP(og.tensor, b * S * D + tt * 128 * D, [[D, 128], [1, D]]))
                s.op('ve', 'tensor_reduce', mu[:], yv, AX.X, ALU.add)
                s.op('act', 'mul', mu[:], mu[:], 1.0 / 64.0)
                s.op('act', 'activation', sqf[:], ybt[:], AF.Square)
                s.op('ve', 'tensor_reduce', msq[:], sqv, AX.X, ALU.add)
                s.op('act', 'mul', msq[:], msq[:], 1.0 / 64.0)
                s.op('ve', 'tensor_tensor', mu2[:], mu[:], mu[:], ALU.mult)
                s.op('ve', 'tensor_tensor', msq[:], msq[:], mu2[:], ALU.subtract)
                s.op('act', 'activation', msq[:], msq[:], AF.Ln, bias=s.cb(0.00064, 128))
                s.op('act', 'activation', rstd[:], msq[:], AF.Exp, scale=-0.5)
                s.op('ve', 'tensor_tensor', ynf[:].rearrange("p (h i) -> p h i", h=16), yv,
                     mu[:].broadcast_to([128, 16, 64]), ALU.subtract)
                s.op('ve', 'tensor_tensor', ynf[:].rearrange("p (h i) -> p h i", h=16),
                     ynf[:].rearrange("p (h i) -> p h i", h=16),
                     rstd[:].broadcast_to([128, 16, 64]), ALU.mult)
                for ct in range(8):
                    s.op('pe', 'transpose', psT[:], ynf[:, ct * 128:(ct + 1) * 128], idsf[:])
                    s.op('ve', 'tensor_scalar', tmpf[:], psT[:], gms[:, ct:ct + 1],
                         bts[:, ct:ct + 1], ALU.mult, ALU.add)
                    s.dma(osf[:, 0:128], rwk[ct * 128:(ct + 1) * 128, b * S + tt * 128: b * S + (tt + 1) * 128])
                    s.op('ve', 'tensor_tensor', tmpf[:], tmpf[:], osf[:, 0:128], ALU.add)
                    s.dma(ybt[:, 0:128], gy[ct * 128:(ct + 1) * 128, b * S + tt * 128: b * S + (tt + 1) * 128])
                    s.op('ve', 'tensor_tensor', ygfm[:, ct * 2048 + tt * 128: ct * 2048 + (tt + 1) * 128],
                         tmpf[:], ybt[:, 0:128], ALU.mult)
            for kt in range(8):
                s.dma(wsb[:, kt * 1024:(kt + 1) * 1024], Wo[kt * 128:(kt + 1) * 128, :])
            for ts in range(4):
                c0 = ts * 512
                for ot in range(8):
                    for kt in range(8):
                        s.mm(ps0[:], wsb[:, kt * 1024 + ot * 128: kt * 1024 + (ot + 1) * 128],
                             ygfm[:, kt * 2048 + c0: kt * 2048 + c0 + 512],
                             start=(kt == 0), stop=(kt == 7))
                    s.op('act', 'copy', osf[:], ps0[:])
                    s.dma(at2[ot * 128:(ot + 1) * 128, b * S + c0: b * S + c0 + 512], osf[:])
        wins = e(nc.sbuf_tensor("wins", [128, 8 * 64], BF16))
        wouts = e(nc.sbuf_tensor("wouts", [64, 1024], BF16))
        bins = e(nc.sbuf_tensor("bins", [64, 1], F32))
        W1s = e(nc.sbuf_tensor("W1s", [64, 16], F32))
        b1s = e(nc.sbuf_tensor("b1s", [64, 16], F32))
        W2s = e(nc.sbuf_tensor("W2s", [64, 16], F32))
        b2s = e(nc.sbuf_tensor("b2s", [64, 1], F32))
        bos = e(nc.sbuf_tensor("bos", [128, 8], F32))
        xsb = e(nc.sbuf_tensor("xsb", [128, 8 * 512], BF16))
        sfb = e(nc.sbuf_tensor("sfb", [64, 512], F32))
        hblk = e(nc.sbuf_tensor("hblk", [64, 512 * 16], F32))
        gsl = e(nc.sbuf_tensor("gsl", [64, 512], BF16))
        psl = e(nc.psum_tensor("psl", [64, 512], F32))
        for nm, dst in ((binc, bins), (W1, W1s), (b1, b1s), (W2, W2s), (b2, b2s), (bout, bos)):
            s.dma(dst[:], nm)
        for kt in range(8):
            s.dma(wins[:, kt * 64:(kt + 1) * 64], win[kt * 128:(kt + 1) * 128, :])
        s.dma(wouts[:], wout)
        h3 = hblk[:].rearrange("p (t i) -> p t i", i=16)
        for ts in range(8):
            c0 = ts * 512
            s.dma(v3(xsb[:], 8), colblock_ap(xnb, D, T, c0, 512))
            for kt in range(8):
                s.mm(psl[:], wins[:, kt * 64:(kt + 1) * 64], xsb[:, kt * 512:(kt + 1) * 512],
                     start=(kt == 0), stop=(kt == 7))
            s.op('act', 'activation', sfb[:], psl[:], AF.Identity, bias=bins[:])
            sfb3 = sfb[:].rearrange("p (t o) -> p t o", o=1).broadcast_to([64, 512, 16])
            W1s3 = W1s[:].rearrange("p (o i) -> p o i", o=1).broadcast_to([64, 512, 16])
            b1s3 = b1s[:].rearrange("p (o i) -> p o i", o=1).broadcast_to([64, 512, 16])
            W2s3 = W2s[:].rearrange("p (o i) -> p o i", o=1).broadcast_to([64, 512, 16])
            s.op('ve', 'tensor_tensor', h3, sfb3, W1s3, ALU.mult)
            s.op('ve', 'tensor_tensor', h3, h3, b1s3, ALU.add)
            s.op('act', 'activation', hblk[:], hblk[:], AF.Relu)
            s.op('ve', 'tensor_tensor', h3, h3, W2s3, ALU.mult)
            s.op('ve', 'tensor_reduce', sfb[:], h3, AX.X, ALU.add)
            s.op('act', 'activation', gsl[:], sfb[:], AF.Identity, bias=b2s[:])
            for ot in range(8):
                s.mm(ps0[:], wouts[:, ot * 128:(ot + 1) * 128], gsl[:])
                s.op('act', 'activation', osf[:], ps0[:], AF.Identity, bias=bos[:, ot:ot + 1])
                s.dma(ff2[ot * 128:(ot + 1) * 128, c0:c0 + 512], osf[:])
        s.fin()
    return nc, stc


def b_l3a():
    nc = bass.Bass("TRN2", target_bir_lowering=False)
    stc = St(nc)
    xnb = stc.inp("xn3b", [D, T], BF16)
    wqkv = stc.inp("l3_wqkvT", [D, 3072], BF16)
    bqkv = stc.inp("l3_bqkv", [128, 24], F32)
    woT = stc.inp("l3_woT", [D, D], BF16)
    wob = stc.inp("l3_wob", [128, 8], F32)
    ghr = stc.inp("l3_ghost", [1, 16], F32)
    o1b = stc.inp("ones1x128b", [1, 128], BF16)
    ocb = stc.inp("onescol128b", [128, 1], BF16)
    idb = stc.inp("id128b", [128, 128], BF16)
    idf = stc.inp("id128f", [128, 128], F32)
    mks = {sh: stc.inp(f"maskL3_{sh}", [128, 512], F32) for sh in (0, 128, 256, 384)}
    mkn = {sh: stc.inp(f"maskL3n_{sh}", [128, 512], F32) for sh in (0, 128, 256, 384)}
    mkT = {sh: stc.inp(f"maskL3T_{sh}", [128, 512], BF16) for sh in (0, 128, 256, 384)}
    at3 = stc.out("at3f", [D, T])
    qkv = stc.internal("qkv3", [3072, T], BF16)
    with contextlib.ExitStack() as stack:
        e = stack.enter_context
        wsb = e(nc.sbuf_tensor("wsb", [128, 8 * 3072], BF16))
        xsb = e(nc.sbuf_tensor("xsb", [128, 8 * 512], BF16))
        osb = e(nc.sbuf_tensor("osb", [128, 512], BF16))
        osf = e(nc.sbuf_tensor("osf", [128, 512], F32))
        bqs = e(nc.sbuf_tensor("bqs", [128, 24], F32))
        wobs = e(nc.sbuf_tensor("wobs", [128, 8], F32))
        ghs = e(nc.sbuf_tensor("ghs", [1, 16], F32))
        qs = e(nc.sbuf_tensor("qs", [64, 2048], BF16))
        ks = e(nc.sbuf_tensor("ks", [64, 2048], BF16))
        vs = e(nc.sbuf_tensor("vs", [64, 2048], BF16))
        vT = e(nc.sbuf_tensor("vT", [128, 16 * 64], BF16))
        stf = e(nc.sbuf_tensor("stf", [128, 512], F32))
        rmx = e(nc.sbuf_tensor("rmx", [128, 1], F32))
        negmc = e(nc.sbuf_tensor("negmc", [128, 16], F32))
        nmT = e(nc.sbuf_tensor("nmT", [16, 128], F32))
        nmTb = e(nc.sbuf_tensor("nmTb", [16, 128], BF16))
        nrow = e(nc.sbuf_tensor("nrow", [1, 2048], BF16))
        zT = e(nc.sbuf_tensor("zT", [128, 16 * 512], BF16))
        drow = e(nc.sbuf_tensor("drow", [1, 512], F32))
        rr = e(nc.sbuf_tensor("rr", [1, 512], BF16))
        rbs = e(nc.sbuf_tensor("rbs", [64, 512], F32))
        ctx = e(nc.sbuf_tensor("ctx", [128, 8 * 2048], BF16))
        mkss = {sh: e(nc.sbuf_tensor(f"mks{sh}", [128, 512], F32)) for sh in (0, 128, 256, 384)}
        mkns = {sh: e(nc.sbuf_tensor(f"mkn{sh}", [128, 512], F32)) for sh in (0, 128, 256, 384)}
        mkTs = {sh: e(nc.sbuf_tensor(f"mkT{sh}", [128, 512], BF16)) for sh in (0, 128, 256, 384)}
        ids = e(nc.sbuf_tensor("ids", [128, 128], BF16))
        idsf = e(nc.sbuf_tensor("idsf", [128, 128], F32))
        onesb = e(nc.sbuf_tensor("onesb", [1, 128], BF16))
        occ = e(nc.sbuf_tensor("occ", [128, 1], BF16))
        ps0 = e(nc.psum_tensor("ps0", [128, 512], F32))
        psd = e(nc.psum_tensor("psd", [1, 512], F32))
        psc = e(nc.psum_tensor("psc", [64, 512], F32))
        psb = e(nc.psum_tensor("psb", [64, 512], F32))
        psT = e(nc.psum_tensor("psT", [16, 128], F32))
        pstb = e(nc.psum_tensor("pstb", [128, 64], BF16))
        s = Seq(nc, stack)
        s.dma(ids[:], idb)
        s.dma(idsf[:], idf)
        s.dma(onesb[:], o1b)
        s.dma(occ[:], ocb)
        s.dma(bqs[:], bqkv)
        s.dma(wobs[:], wob)
        s.dma(ghs[:], ghr)
        for sh in (0, 128, 256, 384):
            s.dma(mkss[sh][:], mks[sh])
            s.dma(mkns[sh][:], mkn[sh])
            s.dma(mkTs[sh][:], mkT[sh])
        for kt in range(8):
            s.dma(wsb[:, kt * 3072:(kt + 1) * 3072], wqkv[kt * 128:(kt + 1) * 128, :])
        for ts in range(8):
            c0 = ts * 512
            s.dma(v3(xsb[:], 8), colblock_ap(xnb, D, T, c0, 512))
            for ot in range(24):
                for kt in range(8):
                    s.mm(ps0[:], wsb[:, kt * 3072 + ot * 128: kt * 3072 + (ot + 1) * 128],
                         xsb[:, kt * 512:(kt + 1) * 512], start=(kt == 0), stop=(kt == 7))
                sc = 0.125 if ot < 8 else 1.0
                s.op('act', 'activation', osb[:], ps0[:], AF.Identity,
                     bias=bqs[:, ot:ot + 1], scale=1.0)
                if ot < 8:
                    s.op('act', 'activation', osb[:], osb[:], AF.Identity, scale=0.125)
                s.dma(qkv[ot * 128:(ot + 1) * 128, c0:c0 + 512], osb[:])
        for b in range(B_PER):
            cb = b * S
            for h in range(16):
                s.dma(qs[:], qkv[h * 64:h * 64 + 64, cb:cb + S])
                s.dma(ks[:], qkv[1024 + h * 64:1024 + h * 64 + 64, cb:cb + S])
                s.dma(vs[:], qkv[2048 + h * 64:2048 + h * 64 + 64, cb:cb + S])
                for tk in range(16):
                    s.op('pe', 'transpose', pstb[:], vs[:, tk * 128:(tk + 1) * 128], ids[0:64, 0:64])
                    s.op('act', 'copy', vT[:, tk * 64:(tk + 1) * 64], pstb[:])
                for tq in range(16):
                    nks = tq // 4 + 1
                    for tks in range(nks):
                        k0 = tks * 512
                        s.mm(ps0[:], qs[:, tq * 128:(tq + 1) * 128], ks[:, k0:k0 + 512])
                        sh = tq * 128 - k0
                        if 0 <= sh <= 384:
                            s.op('ve', 'tensor_tensor', stf[:], ps0[:], mkss[sh][:], ALU.mult)
                            s.op('ve', 'tensor_tensor', stf[:], stf[:], mkns[sh][:], ALU.add)
                        else:
                            s.op('act', 'copy', stf[:], ps0[:])
                        if tks == 0:
                            s.op('ve', 'tensor_reduce', negmc[:, tq:tq + 1], stf[:], AX.X,
                                 ALU.max, negate=True)
                        else:
                            s.op('ve', 'tensor_reduce', rmx[:], stf[:], AX.X, ALU.max, negate=True)
                            s.op('ve', 'tensor_tensor', negmc[:, tq:tq + 1], negmc[:, tq:tq + 1],
                                 rmx[:], ALU.min)
                s.op('pe', 'transpose', psT[:], negmc[:], idsf[:])
                s.op('act', 'copy', nmT[:], psT[:])
                s.op('ve', 'tensor_copy', nmTb[:], nmT[:])
                for tq in range(16):
                    s.dma(nrow[0:1, tq * 128:(tq + 1) * 128], nmTb[tq:tq + 1, :])
                for tqs in range(4):
                    q0 = tqs * 512
                    ntk = 4 * (tqs + 1)
                    for tk in range(ntk):
                        s.mm(ps0[:], ks[:, tk * 128:(tk + 1) * 128], qs[:, q0:q0 + 512],
                             start=True, stop=False)
                        s.mm(ps0[:], onesb[:], nrow[0:1, q0:q0 + 512], start=False, stop=True)
                        zsl = zT[:, tk * 512:(tk + 1) * 512]
                        s.op('act', 'activation', zsl, ps0[:], AF.Exp)
                        sh = tk * 128 - q0
                        if 0 <= sh <= 384:
                            s.op('ve', 'tensor_tensor', zsl, zsl, mkTs[sh][:], ALU.mult)
                        s.mm(psd[:], occ[:], zsl, start=(tk == 0), stop=(tk == ntk - 1))
                    s.op('ve', 'tensor_scalar', drow[:], psd[:], ghs[0:1, h:h + 1], None, ALU.add)
                    s.op('ve', 'reciprocal', drow[:], drow[:])
                    s.op('ve', 'tensor_copy', rr[:], drow[:])
                    for tk in range(ntk):
                        s.mm(psc[:], vT[:, tk * 64:(tk + 1) * 64], zT[:, tk * 512:(tk + 1) * 512],
                             start=(tk == 0), stop=(tk == ntk - 1))
                    s.mm(psb[:], onesb[0:1, 0:64], rr[:])
                    s.op('act', 'copy', rbs[:], psb[:])
                    ct = h // 2
                    rr0 = (h % 2) * 64
                    s.op('ve', 'tensor_tensor',
                         ctx[rr0:rr0 + 64, ct * 2048 + q0: ct * 2048 + q0 + 512],
                         psc[:], rbs[:], ALU.mult)
            for kt in range(8):
                s.dma(wsb[:, kt * 1024:(kt + 1) * 1024], woT[kt * 128:(kt + 1) * 128, :])
            for ts in range(4):
                c0 = ts * 512
                for ot in range(8):
                    for kt in range(8):
                        s.mm(ps0[:], wsb[:, kt * 1024 + ot * 128: kt * 1024 + (ot + 1) * 128],
                             ctx[:, kt * 2048 + c0: kt * 2048 + c0 + 512],
                             start=(kt == 0), stop=(kt == 7))
                    s.op('act', 'activation', osf[:], ps0[:], AF.Identity, bias=wobs[:, ot:ot + 1])
                    s.dma(at3[ot * 128:(ot + 1) * 128, cb + c0: cb + c0 + 512], osf[:])
        s.fin()
    return nc, stc


def b_l3m():
    nc = bass.Bass("TRN2", target_bir_lowering=False)
    stc = St(nc)
    xnb = stc.inp("xn3b", [D, T], BF16)
    xnf = stc.inp("xn3f", [D, T])
    gw = stc.inp("l3_gatewT", [D, 16], F32)
    w1 = stc.inp("l3_w1", [16 * 1024, 512], BF16)
    w3 = stc.inp("l3_w3", [16 * 1024, 512], BF16)
    w2 = stc.inp("l3_w2", [16 * 512, 1024], BF16)
    x3 = stc.inp("x3f", [D, T])
    at3 = stc.inp("at3f", [D, T])
    o1b = stc.inp("ones1x128b", [1, 128], BF16)
    idf = stc.inp("id128f", [128, 128], F32)
    outf = stc.out("outf", [D, T])
    with contextlib.ExitStack() as stack:
        e = stack.enter_context
        xsb = e(nc.sbuf_tensor("xsb", [128, 8 * 512], BF16))
        xsf = e(nc.sbuf_tensor("xsf", [128, 8 * 512], F32))
        gws = e(nc.sbuf_tensor("gws", [128, 8 * 16], F32))
        lgf = e(nc.sbuf_tensor("lgf", [16, 512], F32))
        lgt = e(nc.sbuf_tensor("lgt", [128, 16], F32))
        zz = e(nc.sbuf_tensor("zz", [128, 16], F32))
        zw = e(nc.sbuf_tensor("zw", [128, 16], F32))
        sel = e(nc.sbuf_tensor("sel", [128, 16], F32))
        mi = e(nc.sbuf_tensor("mi", [128, 1], F32))
        s4 = e(nc.sbuf_tensor("s4", [128, 1], F32))
        sct = e(nc.sbuf_tensor("sct", [128, 16], F32))
        scfm = e(nc.sbuf_tensor("scfm", [16, 512], F32))
        srow = e(nc.sbuf_tensor("srow", [1, 512], F32))
        srb = e(nc.sbuf_tensor("srb", [1, 512], BF16))
        w1s = e(nc.sbuf_tensor("w1s", [128, 8 * 512], BF16))
        w3s = e(nc.sbuf_tensor("w3s", [128, 8 * 512], BF16))
        w2s = e(nc.sbuf_tensor("w2s", [128, 4 * 1024], BF16))
        gsl = e(nc.sbuf_tensor("gsl", [128, 4 * 512], BF16))
        usl = e(nc.sbuf_tensor("usl", [128, 512], BF16))
        outacc = e(nc.sbuf_tensor("outacc", [128, 8 * 512], F32))
        osf = e(nc.sbuf_tensor("osf", [128, 512], F32))
        tmpf = e(nc.sbuf_tensor("tmpf", [128, 512], F32))
        idsf = e(nc.sbuf_tensor("idsf", [128, 128], F32))
        onesb = e(nc.sbuf_tensor("onesb", [1, 128], BF16))
        psr = e(nc.psum_tensor("psr", [16, 512], F32))
        psT = e(nc.psum_tensor("psT", [128, 16], F32))
        psT2 = e(nc.psum_tensor("psT2", [16, 128], F32))
        ps1 = e(nc.psum_tensor("ps1", [128, 512], F32))
        ps2 = e(nc.psum_tensor("ps2", [128, 512], F32))
        psS = e(nc.psum_tensor("psS", [128, 512], F32))
        psD = e(nc.psum_tensor("psD", [128, 512], F32))
        s = Seq(nc, stack)
        s.dma(idsf[:], idf)
        s.dma(onesb[:], o1b)
        for kt in range(8):
            s.dma(gws[:, kt * 16:(kt + 1) * 16], gw[kt * 128:(kt + 1) * 128, :])
        for ts in range(8):
            c0 = ts * 512
            s.dma(v3(xsb[:], 8), colblock_ap(xnb, D, T, c0, 512))
            s.dma(v3(xsf[:], 8), colblock_ap(xnf, D, T, c0, 512))
            for kt in range(8):
                s.mm(psr[:], gws[:, kt * 16:(kt + 1) * 16], xsf[:, kt * 512:(kt + 1) * 512],
                     start=(kt == 0), stop=(kt == 7))
            s.op('act', 'copy', lgf[:], psr[:])
            for tt in range(4):
                s.op('pe', 'transpose', psT[:], lgf[:, tt * 128:(tt + 1) * 128], idsf[0:16, 0:16])
                s.op('act', 'copy', lgt[:], psT[:])
                s.op('ve', 'tensor_reduce', mi[:], lgt[:], AX.X, ALU.max, negate=True)
                s.op('act', 'activation', zz[:], lgt[:], AF.Exp, bias=mi[:])
                s.op('ve', 'tensor_copy', zw[:], zz[:])
                for it in range(4):
                    s.op('ve', 'tensor_reduce', mi[:], zw[:], AX.X, ALU.max)
                    if it < 3:
                        s.op('ve', 'tensor_scalar', sel[:], zw[:], mi[:], None, ALU.is_ge)
                        s.op('ve', 'scalar_tensor_tensor', zw[:], sel[:], -2.0, zw[:],
                             ALU.mult, ALU.add)
                s.op('ve', 'tensor_scalar', sel[:], zz[:], mi[:], None, ALU.is_ge)
                s.op('ve', 'tensor_tensor', sct[:], zz[:], sel[:], ALU.mult)
                s.op('ve', 'tensor_reduce', s4[:], sct[:], AX.X, ALU.add)
                s.op('ve', 'reciprocal', s4[:], s4[:])
                s.op('ve', 'tensor_scalar', sct[:], sct[:], s4[:], None, ALU.mult)
                s.op('pe', 'transpose', psT2[:], sct[:], idsf[:])
                s.op('act', 'copy', scfm[:, tt * 128:(tt + 1) * 128], psT2[:])
            for ex in range(16):
                s.dma(v3(w1s[:], 8), bass.AP(w1.tensor, ex * 1024 * 512,
                                             [[512, 128], [128 * 512, 8], [1, 512]]))
                s.dma(v3(w3s[:], 8), bass.AP(w3.tensor, ex * 1024 * 512,
                                             [[512, 128], [128 * 512, 8], [1, 512]]))
                s.dma(v3(w2s[:], 4), bass.AP(w2.tensor, ex * 512 * 1024,
                                             [[1024, 128], [128 * 1024, 4], [1, 1024]]))
                s.dma(srow[:], scfm[ex:ex + 1, :])
                s.op('ve', 'tensor_copy', srb[:], srow[:])
                s.mm(psS[:], onesb[:], srb[:])
                for it in range(4):
                    for kt in range(8):
                        s.mm(ps1[:], w1s[:, kt * 512 + it * 128: kt * 512 + (it + 1) * 128],
                             xsb[:, kt * 512:(kt + 1) * 512], start=(kt == 0), stop=(kt == 7))
                    for kt in range(8):
                        s.mm(ps2[:], w3s[:, kt * 512 + it * 128: kt * 512 + (it + 1) * 128],
                             xsb[:, kt * 512:(kt + 1) * 512], start=(kt == 0), stop=(kt == 7))
                    s.op('act', 'activation', gsl[:, it * 512:(it + 1) * 512], ps1[:], AF.Silu)
                    s.op('act', 'copy', usl[:], ps2[:])
                    s.op('ve', 'tensor_tensor', gsl[:, it * 512:(it + 1) * 512],
                         gsl[:, it * 512:(it + 1) * 512], usl[:], ALU.mult)
                    s.op('ve', 'tensor_tensor', gsl[:, it * 512:(it + 1) * 512],
                         gsl[:, it * 512:(it + 1) * 512], psS[:], ALU.mult)
                for dt in range(8):
                    for it in range(4):
                        s.mm(psD[:], w2s[:, it * 1024 + dt * 128: it * 1024 + (dt + 1) * 128],
                             gsl[:, it * 512:(it + 1) * 512], start=(it == 0), stop=(it == 3))
                    if ex == 0:
                        s.op('act', 'copy', outacc[:, dt * 512:(dt + 1) * 512], psD[:])
                    else:
                        s.op('ve', 'tensor_tensor', outacc[:, dt * 512:(dt + 1) * 512],
                             outacc[:, dt * 512:(dt + 1) * 512], psD[:], ALU.add)
            for dt in range(8):
                s.dma(osf[:], x3[dt * 128:(dt + 1) * 128, c0:c0 + 512])
                s.dma(tmpf[:], at3[dt * 128:(dt + 1) * 128, c0:c0 + 512])
                s.op('ve', 'tensor_tensor', osf[:], osf[:], tmpf[:], ALU.add)
                s.op('ve', 'tensor_tensor', osf[:], osf[:],
                     outacc[:, dt * 512:(dt + 1) * 512], ALU.add)
                s.dma(outf[dt * 128:(dt + 1) * 128, c0:c0 + 512], osf[:])
        s.fin()
    return nc, stc


_PROGS = {}


def _get(key, fn, *args):
    if key not in _PROGS:
        _PROGS[key] = fn(*args)
    return _PROGS[key]


def _run(key, fn, args, percore, W):
    nc, stc = _get(key, fn, *args)
    in_maps = []
    for c in range(NCORE):
        m = {}
        for n in stc.inputs:
            if n in percore[c]:
                m[n] = percore[c][n]
            elif n in W:
                m[n] = W[n]
            else:
                m[n] = STATICS[n]
        in_maps.append(m)
    res = run_bass_kernel_spmd(nc, in_maps, core_ids=list(range(NCORE)))
    for c in range(NCORE):
        for n in stc.outputs:
            percore[c][n] = np.asarray(res.results[c][n])
    return res


def _prep_weights(ip):
    W = {}
    W['l0_wq'] = bf(ip['L0_attn_w_q']); W['l0_wk'] = bf(ip['L0_attn_w_k'])
    W['l0_wv'] = bf(ip['L0_attn_w_v']); W['l0_wo'] = bf(ip['L0_attn_w_o'])
    W['l0_table'] = f32(ip['L0_attn_rel_bias_table'])
    W['l0_hyp'] = f32(ip['L0_ffn_hyperplanes'])
    W['l0_wex'] = bf(np.asarray(ip['L0_ffn_expert_weights']).reshape(64, -1))
    ipw = f32(ip['L1_attn_in_proj_w'])
    W['l1_inprojT'] = bf(ipw.T)
    W['l1_inprojdtT'] = f32(ipw.T[:, 4352:4384])
    W['l1_convw'] = f32(np.asarray(ip['L1_attn_conv_w'])[:, 0, :])
    W['l1_convb'] = f32(np.asarray(ip['L1_attn_conv_b']).reshape(-1, 1))
    W['l1_dtbias'] = f32(np.asarray(ip['L1_attn_dt_bias']).reshape(32, 1))
    W['l1_alog'] = f32(np.asarray(ip['L1_attn_A_log']).reshape(32, 1))
    W['l1_drow'] = f32(np.asarray(ip['L1_attn_D']).reshape(1, 32))
    W['l1_normw'] = f32(np.asarray(ip['L1_attn_norm_w']).reshape(16, 128).T)
    W['l1_outprojT'] = bf(np.asarray(ip['L1_attn_out_proj_w']).T)
    W['l1f_w1T'] = bf(np.asarray(ip['L1_ffn_w1_w']).T)
    W['l1f_w2T'] = bf(np.asarray(ip['L1_ffn_w2_w']).T)
    W['l1f_h1T'] = bf(np.asarray(ip['L1_ffn_h1_w']).T)
    W['l1f_h2T'] = bf(np.asarray(ip['L1_ffn_h2_w']).T)
    W['l1f_w3T'] = bf(np.asarray(ip['L1_ffn_w3_w']).T)
    W['l1f_s1'] = f32(np.asarray(ip['L1_ffn_w1_s']).reshape(32, 128).T)
    W['l1f_s2'] = f32(np.asarray(ip['L1_ffn_w2_s']).reshape(32, 128).T)
    W['l1f_sh1'] = f32(np.asarray(ip['L1_ffn_h1_s']).reshape(32, 128).T)
    W['l1f_sh2'] = f32(np.asarray(ip['L1_ffn_h2_s']).reshape(32, 128).T)
    W['l1f_s3'] = f32(np.asarray(ip['L1_ffn_w3_s']).reshape(8, 128).T)
    for p in "rwkvag":
        W[f'l2_mx_{p}'] = f32(np.asarray(ip[f'L2_attn_x_{p}']).reshape(D, 1))
    W['l2_Wr'] = bf(ip['L2_attn_Wr']); W['l2_Wk'] = bf(ip['L2_attn_Wk'])
    W['l2_Wv'] = bf(ip['L2_attn_Wv']); W['l2_Wo'] = bf(ip['L2_attn_Wo'])
    W['l2_w0'] = f32(np.asarray(ip['L2_attn_w0']).reshape(D, 1))
    W['l2_w1'] = bf(ip['L2_attn_w1']); W['l2_w2'] = bf(ip['L2_attn_w2'])
    W['l2_a0'] = f32(np.asarray(ip['L2_attn_a0']).reshape(D, 1))
    W['l2_a1'] = bf(ip['L2_attn_a1']); W['l2_a2'] = bf(ip['L2_attn_a2'])
    W['l2_g1'] = bf(ip['L2_attn_g1']); W['l2_g2'] = bf(ip['L2_attn_g2'])
    W['l2_kk'] = f32(np.asarray(ip['L2_attn_k_k']).reshape(D, 1))
    W['l2_ka'] = f32(np.asarray(ip['L2_attn_k_a']).reshape(D, 1))
    W['l2_rk'] = f32(np.asarray(ip['L2_attn_r_k']).reshape(D, 1))
    W['l2_gam'] = f32(np.asarray(ip['L2_attn_ln_gamma']).reshape(8, 128).T)
    W['l2_bet'] = f32(np.asarray(ip['L2_attn_ln_beta']).reshape(8, 128).T)
    W['l2f_win'] = bf(ip['L2_ffn_Win'])
    W['l2f_bin'] = f32(np.asarray(ip['L2_ffn_bin_']).reshape(64, 1))
    W['l2f_W1'] = f32(ip['L2_ffn_W1']); W['l2f_b1'] = f32(ip['L2_ffn_b1'])
    W['l2f_W2'] = f32(ip['L2_ffn_W2'])
    W['l2f_b2'] = f32(np.asarray(ip['L2_ffn_b2']).reshape(64, 1))
    W['l2f_wout'] = bf(ip['L2_ffn_Wout'])
    W['l2f_bout'] = f32(np.asarray(ip['L2_ffn_bout']).reshape(8, 128).T)
    W['l3_wqkvT'] = bf(np.asarray(ip['L3_attn_Wqkv_w']).T)
    W['l3_bqkv'] = f32(np.asarray(ip['L3_attn_Wqkv_b']).reshape(24, 128).T)
    W['l3_woT'] = bf(np.asarray(ip['L3_attn_Wo_w']).T)
    W['l3_wob'] = f32(np.asarray(ip['L3_attn_Wo_b']).reshape(8, 128).T)
    W['l3_ghost'] = f32(np.asarray(ip['L3_attn_ghost']).reshape(1, 16))
    W['l3_gatewT'] = f32(np.asarray(ip['L3_ffn_gate_w']).T)
    W['l3_w1'] = bf(np.asarray(ip['L3_ffn_w1']).reshape(16 * 1024, 512))
    W['l3_w3'] = bf(np.asarray(ip['L3_ffn_w3']).reshape(16 * 1024, 512))
    W['l3_w2'] = bf(np.asarray(ip['L3_ffn_w2']).reshape(16 * 512, 1024))
    return W


def kernel(**ip):
    x = f32(ip['L0_attn_hidden_states'])
    W = _prep_weights(ip)
    percore = [dict() for _ in range(NCORE)]
    for c in range(NCORE):
        percore[c]['x0f'] = np.ascontiguousarray(
            np.concatenate([x[2 * c].T, x[2 * c + 1].T], axis=1))
    _run('ln0', b_ln, (['x0f'], None, 'xn0f', 'xn0b'), percore, W)
    _run('l0a', b_l0a, (), percore, W)
    _run('l0f', b_l0f, (), percore, W)
    _run('ln1', b_ln, (['x0f', 'at0f', 'ff0f'], 'x1f', 'xn1f', 'xn1b'), percore, W)
    _run('l1a', b_l1a, (), percore, W)
    _run('l1f', b_l1f, (), percore, W)
    _run('ln2', b_ln, (['x1f', 'at1f', 'ff1f'], 'x2f', 'xn2f', 'xn2b'), percore, W)
    _run('l2p', b_l2p, (), percore, W)
    osegs = [[] for _ in range(NCORE)]
    for c in range(NCORE):
        percore[c]['seg_sin'] = np.zeros((128, D), np.float32)
    for seg in range(4):
        for c in range(NCORE):
            pc = percore[c]
            for nm in ('r', 'w', 'k', 'a', 'b'):
                arr = pc['str' + nm].reshape(128, 16, S)[:, :, seg * 512:(seg + 1) * 512]
                pc['seg_' + nm] = np.ascontiguousarray(arr).reshape(128, 16 * 512)
            vv = pc['strv'].reshape(2, S, D)[:, seg * 512:(seg + 1) * 512, :]
            pc['seg_v'] = np.ascontiguousarray(vv).reshape(2, 512 * D)
        _run('l2s', b_l2s, (), percore, W)
        for c in range(NCORE):
            osegs[c].append(percore[c]['seg_o'].reshape(2, 512, D))
            percore[c]['seg_sin'] = percore[c]['seg_sout']
    for c in range(NCORE):
        percore[c]['oall'] = np.ascontiguousarray(
            np.concatenate(osegs[c], axis=1)).reshape(2, S * D)
    _run('l2q', b_l2q, (), percore, W)
    _run('ln3', b_ln, (['x2f', 'at2f', 'ff2f'], 'x3f', 'xn3f', 'xn3b'), percore, W)
    _run('l3a', b_l3a, (), percore, W)
    _run('l3m', b_l3m, (), percore, W)
    out = np.zeros((16, S, D), np.float32)
    for c in range(NCORE):
        of = percore[c]['outf']
        out[2 * c] = of[:, 0:S].T
        out[2 * c + 1] = of[:, S:2 * S].T
    return out
